# version 13
# baseline (speedup 1.0000x reference)
import os
import numpy as np
from collections import defaultdict
from contextlib import ExitStack

_PH = os.environ.get("ATH_PHASE", "full")
_NT_LIM = int(os.environ.get("ATH_NT", "4"))
_ODM = int(os.environ.get("ATH_ODM", "32"))
_OSUB = os.environ.get("ATH_OSUB", "full")

import concourse.bass as bass
import concourse.mybir as mybir
from concourse.bass_utils import run_bass_kernel_spmd

F32 = mybir.dt.float32
BF16 = mybir.dt.bfloat16
AF = mybir.ActivationFunctionType
ALU = mybir.AluOpType
AX = mybir.AxisListType

DM = 4096
DI = 8192
NH = 128
PDim = 64
NG = 8
NS = 128
CK = 4
CH = 256
L = 2048
TT = 512
NT = L // TT
NCH = TT // CH
NK = DM // 128
NF = 145
NFD = DI // 128
CONVD = DI + 2 * NG * NS
NCT = CONVD // 128
XBCF0 = 64
DTF = 144
EPS = 1e-5

ENGINES = ("sync", "gpsimd", "scalar", "vector", "tensor")


class Plan:
    def __init__(self):
        self.steps = {e: [] for e in ENGINES}
        self.cnt = defaultdict(int)

    def add(self, engine, fn, waits=(), inc=None, incn=1):
        tok = None
        if inc is not None:
            self.cnt[inc] += incn
            tok = (inc, self.cnt[inc])
        ws = tuple(w for w in waits if w is not None)
        for (s, v) in ws:
            assert v <= self.cnt[s], f"wait on future token {s}={v}>{self.cnt[s]}"
        self.steps[engine].append((fn, ws, (inc, incn) if inc else None))
        return tok


def build_program():
    nc = bass.Bass("TRN2", target_bir_lowering=False)

    hs = nc.dram_tensor("hs", [L, DM], F32, kind="ExternalInput").ap()
    w_in = nc.dram_tensor("L0_attn_in_proj_w", [NF * 128, DM], F32, kind="ExternalInput").ap()
    conv_w = nc.dram_tensor("L0_attn_conv_w", [CONVD, CK], F32, kind="ExternalInput").ap()
    conv_b = nc.dram_tensor("L0_attn_conv_b", [CONVD], F32, kind="ExternalInput").ap()
    dt_bias = nc.dram_tensor("L0_attn_dt_bias", [NH], F32, kind="ExternalInput").ap()
    a_log = nc.dram_tensor("L0_attn_A_log", [NH], F32, kind="ExternalInput").ap()
    d_in = nc.dram_tensor("L0_attn_D", [NH], F32, kind="ExternalInput").ap()
    norm_w = nc.dram_tensor("L0_attn_norm_w", [DI], F32, kind="ExternalInput").ap()
    w_out = nc.dram_tensor("L0_attn_out_proj_w", [DM, DI], F32, kind="ExternalInput").ap()
    w_r = nc.dram_tensor("L0_ffn_router_w", [4, DM], F32, kind="ExternalInput").ap()
    ffn_a = nc.dram_tensor("L0_ffn_A", [4, 16, DM], F32, kind="ExternalInput").ap()
    ffn_b = nc.dram_tensor("L0_ffn_B", [4, DM, 16], F32, kind="ExternalInput").ap()
    out_h = nc.dram_tensor("out", [L, DM], F32, kind="ExternalOutput").ap()

    win_cache = nc.dram_tensor("win_cache", [NF, 128, DM], BF16, kind="Internal").ap()
    wout_cache = nc.dram_tensor("wout_cache", [NK, 128, DI], BF16, kind="Internal").ap()
    acs_d = nc.dram_tensor("acs_d", [2, 2, NH, CH], BF16, kind="Internal").ap()
    row_d = nc.dram_tensor("row_d", [2, 4096], BF16, kind="Internal").ap()
    ffn_d = nc.dram_tensor("ffn_d", [128, 2048 + 4096 + 128], BF16, kind="Internal").ap()
    cwcb_d = nc.dram_tensor("cwcb_d", [128, 400], F32, kind="Internal").ap()

    P = Plan()
    with ExitStack() as ctx:
        alloc = nc.alloc_sbuf_tensor
        r0 = alloc("r0_xbc", [128, NCT * TT], BF16).ap()
        r1 = alloc("r1_xt", [128, NK * TT], BF16).ap()
        r2 = alloc("r2_big", [128, 16384], BF16).ap()
        r3 = alloc("r3_w", [128, 8192], BF16).ap()
        r4 = alloc("r4_s", [128, NH * PDim], BF16).ap()
        r5 = alloc("r5_a", [128, 4096], BF16).ap()
        r6 = alloc("r6_b", [128, 4096], F32).ap()
        dtproc = alloc("dtproc", [128, TT], F32).ap()
        tailb = alloc("tailb", [128, NCT * 3], BF16).ap()
        zsb = alloc("zsb", [128, 2 * TT], BF16).ap()
        opfm = zsb
        e4 = alloc("e4", [4, 64], BF16).ap()
        identb = alloc("identb", [128, 128], BF16).ap()
        identf = alloc("identf", [128, 128], F32).ap()
        trimask = alloc("trimask", [128, 128], BF16).ap()
        onescol = alloc("onescol", [128, 1], BF16).ap()
        onesf1 = alloc("onesf1", [1, 128], F32).ap()
        nw_col = alloc("nw_col", [128, NFD], F32).ap()
        dpairs = alloc("dpairs", [128, PDim], F32).ap()
        acol = alloc("acol", [128, 1], F32).ap()
        dtbcol = alloc("dtbcol", [128, 1], F32).ap()

        smallf = alloc("smallf", [128, 64], F32).ap()
        logits = alloc("logits", [128, 16], F32).ap()
        gatesb = alloc("gatesb", [128, 16], F32).ap()
        cmp4 = alloc("cmp4", [128, 16], F32).ap()
        cnt4 = alloc("cnt4", [128, 4], F32).ap()
        jlt = alloc("jlt", [128, 16], F32).ap()
        invc = alloc("invc", [128, 4], F32).ap()
        constf = alloc("constf", [128, 8], F32).ap()

        rowbuf = r5[0:1, 0:4096]
        dtt = r6[:, 0:512]
        rmsrow = dtt[0:1, 0:TT]
        onesf256 = r6[:, 3840:4096]
        g2b = r6.bitcast(BF16)[:, 1024:1536]
        g2b2 = [g2b, r6.bitcast(BF16)[:, 1536:2048]]
        rraw = dtproc[0:4, 0:TT]
        gatesT = rraw
        midb = zsb[0:64, 0:TT]
        gatesTb = midb[0:4, 0:TT]
        gmidb = dtproc.bitcast(BF16)[0:64, 0:TT]
        dttT = r5.bitcast(F32)[:, 1024:1280]
        dtdsT = r5.bitcast(F32)[:, 1280:1536]
        cdb = r5.bitcast(F32)[:, 1536:1664]
        dball = cdb
        cdrow = r5.bitcast(F32)[0:1, 1664:1792]
        cw_all = r5.bitcast(F32)[:, 0:320]
        cb_col = r5.bitcast(F32)[:, 320:400]
        sp1x = r5.bitcast(F32)[:, 400:912]
        sp2x = r5.bitcast(F32)[:, 912:1424]
        acat = r3[:, 0:2048]
        bcat = r3[0:64, 2048:6144]
        wrt = r3[:, 6144:6272]
        xt = r1
        xtk = xt.rearrange("p (k t) -> p k t", t=TT)
        lnm = [r1[:, i * 4096:(i + 1) * 4096] for i in range(4)]
        r2f = r2.bitcast(F32)
        wnat32 = [r2f[:, 0:4096], r2f[:, 4096:8192]]
        dhl = [r2[0:4, 0:4096], r2[0:4, 4096:8192]]
        rhs4 = [r2[0:4, 8192:12288], r2[0:4, 12288:16384]]
        woasm = [r2[:, 0:8192], r2[:, 8192:16384]]
        x1fm = r2
        x1fmk = x1fm.rearrange("p (k t) -> p k t", t=TT)
        winT = [r3[:, 0:4096], r3[:, 4096:8192]]
        r3f = r3.bitcast(F32)
        wnatq = [r3f[:, 0:2048], r3f[:, 2048:4096]]
        mxb = [r3[:, i * 256:(i + 1) * 256] for i in range(2)]
        xdtb = [r3[:, 512 + i * 256: 512 + (i + 1) * 256] for i in range(2)]
        dsxdtb = [r3[:, 1024 + i * 256: 1024 + (i + 1) * 256] for i in range(2)]
        r5f = r5.bitcast(F32)
        acsb = r5f[:, 0:256]
        dab = r5f[:, 256:512]
        dsb = r5f[:, 512:768]
        dtdsb = r5f[:, 768:1024]
        ln2b = r5
        r6b = r6.bitcast(BF16)
        gg = [r6b[:, g * 384:(g + 1) * 384] for g in range(NG)]
        btm = [r6b[:, 3072 + g * 256: 3072 + (g + 1) * 256] for g in range(NG)]
        acshi = r6b[:, 5120:5376]
        acslo = r6b[:, 5376:5632]
        lbuf = [r6b[:, 5632 + i * 384: 5632 + (i + 1) * 384] for i in range(2)]
        odbb = [r6b[:, 6400 + i * 256: 6400 + (i + 1) * 256] for i in range(2)]
        mpb = [r6b[:, 6912 + i * 384: 6912 + (i + 1) * 384] for i in range(2)]
        mpb2 = [mpb, [zsb[:, 0:384], zsb[:, 384:768]]]
        mxb2 = [mxb, [zsb[:, 768:1024], r6b[:, 5376:5632]]]
        opfm2 = [zsb[:, 0:512], zsb[:, 512:1024]]
        convtmp = r6[:, 0:512]
        xstage = [r6b[:, 0:4096], r6b[:, 4096:8192]]
        xres = r6
        obuf = r6
        r0j = r0.rearrange("p (j t) -> p j t", t=TT)

        apt = nc.alloc_psum_tensor
        pm = [apt("pm0", [128, 512], F32).ap(), apt("pm1", [128, 512], F32).ap()]
        pt = [apt("pt0", [128, 512], F32).ap(), apt("pt1", [128, 512], F32).ap()]
        pdk = [apt("pd0", [128, 512], F32).ap(), apt("pd1", [128, 512], F32).ap()]
        py = [apt("py0", [128, 512], F32).ap(), apt("py1", [128, 512], F32).ap()]
        ptb = [p.bitcast(BF16) for p in pt]
        pdb = [p.bitcast(BF16) for p in pdk]

        semn = ["te", "dve", "act", "gps",
                "rowd", "cr", "sm", "cwcb", "prep", "ffn",
                "xs0", "xs1", "w0", "w1", "cwin0", "cwin1",
                "img0", "img1", "st0", "st1",
                "wq0", "wq1", "wo0", "wo1", "cwout0", "cwout1",
                "w2", "w3", "wo2", "wo3",
                "xr", "ob"]
        sems = {n: ctx.enter_context(nc.semaphore(n)) for n in semn}
        blk = ctx.enter_context(nc.Block())

        def dma(queue, stream, out, in_, waits=(), **kw):
            def fn(eng, out=out, in_=in_, kw=kw):
                return eng.dma_start(out=out, in_=in_, **kw)
            return P.add(queue, fn, waits=waits, inc=stream, incn=16)

        def op(engine, meth, *args, waits=(), inc=True, **kw):
            sem = {"tensor": "te", "vector": "dve", "scalar": "act",
                   "gpsimd": "gps"}[engine]
            def fn(eng, meth=meth, args=args, kw=kw):
                return getattr(eng, meth)(*args, **kw)
            selfw = ((sem, P.cnt[sem]),) if P.cnt[sem] else ()
            return P.add(engine, fn, waits=tuple(waits) + selfw, inc=sem)

        def mm(out, lhsT, rhsX, waits=(), inc=True, start=True, stop=True, tp=None,
               skip=False):
            kw = dict(start=start, stop=stop)
            if tp is not None:
                kw["tile_position"] = tp
            if skip:
                kw["skip_group_check"] = True
            def fn(eng, out=out, lhsT=lhsT, rhsX=rhsX, kw=kw):
                return eng.matmul(out, lhsT, rhsX, **kw)
            return P.add("tensor", fn, waits=waits, inc="te" if inc else None)

        def tr(out, in_, waits=(), inc=True):
            p = in_.shape[0]
            idt = identf if in_.dtype == F32 else identb
            def fn(eng, out=out, in_=in_, p=p, idt=idt):
                return eng.transpose(out, in_, idt[0:p, 0:p])
            return P.add("tensor", fn, waits=waits, inc="te" if inc else None)

        tk = op("gpsimd", "memset", identb, 1.0)
        tk = op("gpsimd", "affine_select", out=identb, in_=identb,
                compare_op=ALU.is_ge, fill=0.0, base=0, pattern=[[-1, 128]],
                channel_multiplier=1, waits=[tk])
        tk = op("gpsimd", "affine_select", out=identb, in_=identb,
                compare_op=ALU.is_ge, fill=0.0, base=0, pattern=[[1, 128]],
                channel_multiplier=-1, waits=[tk])
        tk = op("gpsimd", "tensor_copy", identf, identb, waits=[tk])
        tk = op("gpsimd", "memset", trimask, 0.0, waits=[tk])
        tk = op("gpsimd", "affine_select", out=trimask, in_=trimask,
                compare_op=ALU.is_ge, fill=-1e30, base=0, pattern=[[1, 128]],
                channel_multiplier=-1, waits=[tk])
        tk = op("gpsimd", "memset", e4, 1.0, waits=[tk])
        tk = op("gpsimd", "affine_select", out=e4, in_=e4, compare_op=ALU.is_ge,
                fill=0.0, base=0, pattern=[[1, 64]], channel_multiplier=-16,
                waits=[tk])
        tk = op("gpsimd", "affine_select", out=e4, in_=e4, compare_op=ALU.is_ge,
                fill=0.0, base=15, pattern=[[-1, 64]], channel_multiplier=16,
                waits=[tk])
        op("gpsimd", "memset", onescol, 1.0, inc=False)
        op("gpsimd", "memset", onesf1, 1.0, inc=False)
        op("gpsimd", "memset", r4, 0.0, inc=False)
        op("gpsimd", "memset", tailb, 0.0, inc=False)
        for e in range(4):
            for j in range(4):
                op("gpsimd", "memset", jlt[:, e * 4 + j: e * 4 + j + 1],
                   1.0 if j < e else 0.0, inc=False)
        cvals = [1.0, -1.0, -0.5, 4.0, EPS, 1.0 / DI, 2.0, 0.5]
        for ci, cv in enumerate(cvals):
            op("gpsimd", "memset", constf[:, ci:ci + 1], cv, inc=False)
            nc.const_aps.register(F32, cv, constf[:, ci:ci + 1])
        tok_setup = op("gpsimd", "memset", smallf, 0.0, waits=[tk])

        op("vector", "memset", rowbuf[0:1, 0:2048], 1.0, inc=False)
        t_row1 = op("vector", "memset", rowbuf[0:1, 2048:4096], -1.0)
        t_rowd = dma("sync", "rowd", row_d[0:1, 0:2048], rowbuf[0:1, 0:2048],
                     waits=[t_row1])
        t_rowd = dma("sync", "rowd", row_d[0:1, 2048:4096], rowbuf[0:1, 0:2048],
                     waits=[t_rowd])
        t_rowd = dma("sync", "rowd", row_d[1:2, 0:2048], rowbuf[0:1, 2048:4096],
                     waits=[t_rowd])
        t_rowd = dma("sync", "rowd", row_d[1:2, 2048:4096], rowbuf[0:1, 2048:4096],
                     waits=[t_rowd])

        t_dtb = dma("scalar", "sm", dtbcol, dt_bias.unsqueeze(1))
        t_al = dma("scalar", "sm", acol, a_log.unsqueeze(1), waits=[t_dtb])
        t_drow = dma("scalar", "sm", rowbuf.bitcast(F32)[0:1, 0:128],
                     d_in.unsqueeze(0), waits=[t_rowd, t_al])
        t_aexp = op("scalar", "activation", out=acol, in_=acol, func=AF.Exp,
                    waits=[t_al])
        t_aneg = op("vector", "tensor_scalar_mul", acol, acol, -1.0, waits=[t_aexp])
        t_dmm = mm(pm[0][:, 0:128], onesf1, rowbuf.bitcast(F32)[0:1, 0:128],
                   waits=[t_drow, tok_setup])
        t_dball = op("vector", "tensor_copy", dball, pm[0][:, 0:128], waits=[t_dmm])
        t_dp1 = op("vector", "tensor_copy", dpairs[0:64, :], dball[0:64, 0:128:2],
                   waits=[t_dball])
        t_dp2 = op("vector", "tensor_copy", dpairs[64:128, :], dball[64:128, 1:128:2],
                   waits=[t_dp1])
        t_cw = t_dmm
        for part in range(4):
            t_cw = dma("sync", "sm",
                       cw_all[32 * part:32 * part + 32, :].rearrange(
                           "p (i k) -> p i k", k=CK),
                       conv_w.rearrange("(i q p) k -> q p i k", q=4, p=32)[part],
                       waits=[t_cw], allow_slow_non_contiguous=True)
        t_cbn = dma("sync", "sm", r6[0:80, 0:128],
                    conv_b.rearrange("(i p) -> i p", p=128), waits=[t_dp2, t_cw])
        t_cbt = tr(pm[1][:, 0:80], r6[0:80, 0:128], waits=[t_cbn, tok_setup])
        t_cb = op("vector", "tensor_copy", cb_col, pm[1][:, 0:80], waits=[t_cbt])
        t_cwcb = dma("sync", "cwcb", cwcb_d, r5f[:, 0:400], waits=[t_cw, t_cb])
        t_nwn = dma("sync", "sm", r6[0:64, 128:256],
                    norm_w.rearrange("(i p) -> i p", p=128), waits=[t_cb, t_cw])
        t_nwt = tr(pm[1][:, 128:192], r6[0:64, 128:256], waits=[t_nwn, t_cb])
        t_nw = op("vector", "tensor_copy", nw_col, pm[1][:, 128:192], waits=[t_nwt])

        t_an = dma("gpsimd", "prep", r6b[0:64, 0:4096],
                   ffn_a.rearrange("e r d -> (e r) d"), waits=[t_nw])
        prev_ev = t_nw
        for k in range(NK):
            tk = tr(ptb[0][:, 0:64], r6b[0:64, k * 128:(k + 1) * 128],
                    waits=[t_an, prev_ev])
            prev_ev = op("vector", "tensor_copy", acat[:, k * 64:(k + 1) * 64],
                         ptb[0][:, 0:64], waits=[tk])
        t_wrn = dma("gpsimd", "prep", r6b[0:4, 4096:8192], w_r,
                     waits=[prev_ev, t_an])
        for k in range(NK):
            tk = tr(ptb[0][:, 0:4], r6b[0:4, 4096 + k * 128: 4096 + (k + 1) * 128],
                    waits=[t_wrn, prev_ev])
            prev_ev = op("vector", "tensor_copy", wrt[:, k * 4:(k + 1) * 4],
                         ptb[0][:, 0:4], waits=[tk])
        t_bn_prev = prev_ev
        for oc in range(NK):
            src = ffn_b.rearrange("e (oc p) r -> oc p e r", p=128)[oc]
            tn = dma("gpsimd", "prep",
                     r6b[0:128, 0:64].rearrange("p (e r) -> p e r", r=16),
                     src, waits=[t_bn_prev, t_wrn])
            tk = tr(ptb[0][0:64, 0:128], r6b[0:128, 0:64],
                    waits=[tn, prev_ev])
            prev_ev = op("scalar", "mul", bcat[:, oc * 128:(oc + 1) * 128],
                         ptb[0][0:64, 0:128], 4.0, waits=[tk])
            t_bn_prev = prev_ev
        t_ffns = dma("sync", "ffn", ffn_d[:, 0:2048], acat, waits=[prev_ev])
        t_ffns = dma("sync", "ffn", ffn_d[0:64, 2048:6144], bcat, waits=[t_ffns])
        t_ffns = dma("sync", "ffn", ffn_d[:, 6144:6272], wrt, waits=[t_ffns])
        t_prep_done = op("vector", "engine_nop", waits=[prev_ev, t_ffns])

        tok_r0_free = None
        tok_r1_free = None
        tok_r2_free = None
        tok_r3_free = t_prep_done
        tok_r5_free = None
        tok_r6_free = None
        tok_pm = [t_dball, t_nw]
        t_refill_last = [None, None]
        t_batch_mm = [None] * 8
        t_supd_prev = [None] * 64
        t_out_last = None
        t_prev_obuf = None
        t_cache_win = [None, None]
        t_cache_wout = [None, None]

        for tt in range(min(NT, _NT_LIM)):
            pass1 = (tt == 0)
            c0 = tt * TT
            last_tile = (tt == min(NT, _NT_LIM) - 1)

            t_xtev = []
            t_xload = []
            for ts in range(4):
                w = []
                if tt == 0 and ts == 0:
                    w.append(t_prep_done)
                if ts == 0:
                    if tok_r1_free is not None:
                        w.append(tok_r1_free)
                    if tok_r6_free is not None:
                        w.append(tok_r6_free)
                if len(t_xload) >= 2:
                    w.append(t_xtev[-1])
                tn = dma("gpsimd", "xs%d" % (ts % 2), xstage[ts % 2],
                         hs[c0 + ts * 128: c0 + (ts + 1) * 128, :], waits=w)
                t_xload.append(tn)
                for kg in range(8):
                    tts = []
                    for k4 in range(4):
                        k = kg * 4 + k4
                        w2 = [tn]
                        if len(t_xtev) >= 2 and k4 == 0:
                            w2.append(t_xtev[-2])
                        tts.append(tr(ptb[kg % 2][:, k4 * 128:(k4 + 1) * 128],
                                      xstage[ts % 2][:, k * 128:(k + 1) * 128],
                                      waits=w2))
                    t_xtev.append(op(
                        "vector", "tensor_copy",
                        xtk[:, kg * 4:(kg + 1) * 4, ts * 128:(ts + 1) * 128],
                        ptb[kg % 2].rearrange("p (k c) -> p k c", c=128)[:, 0:4, :],
                        waits=tts))
            t_xt_done = t_xtev[-1]

            def acquire_win(f, buf, wprev, extra=()):
                if pass1:
                    w = list(extra) + ([wprev] if wprev else [])
                    tn = dma("sync", "w%d" % buf, wnat32[buf],
                             w_in[f * 128:(f + 1) * 128, :], waits=w)
                    evs = []
                    for kg in range(8):
                        tts = []
                        for k4 in range(4):
                            k = kg * 4 + k4
                            w2 = [tn]
                            if len(evs) >= 2 and k4 == 0:
                                w2.append(evs[-2])
                            tts.append(tr(pdk[kg % 2][:, k4 * 128:(k4 + 1) * 128],
                                          wnat32[buf][:, k * 128:(k + 1) * 128],
                                          waits=w2))
                        evw = list(tts)
                        if kg == 0 and t_cache_win[buf] is not None:
                            evw.append(t_cache_win[buf])
                        evs.append(op(
                            "vector", "tensor_copy",
                            winT[buf].rearrange("p (k c) -> p k c", c=128)[:, kg * 4:(kg + 1) * 4, :],
                            pdk[kg % 2].rearrange("p (k c) -> p k c", c=128)[:, 0:4, :],
                            waits=evw))
                    t_cache_win[buf] = dma("sync", "cwin%d" % buf, win_cache[f],
                                           winT[buf], waits=[evs[-1]])
                    return evs[-1]
                else:
                    w = list(extra) + ([wprev] if wprev else [])
                    ta = dma("sync", "w%d" % buf, winT[buf][:, 0:2048],
                             win_cache[f][:, 0:2048], waits=w)
                    tb = dma("sync", "w%d" % (buf + 2),
                             winT[buf][:, 2048:4096],
                             win_cache[f][:, 2048:4096], waits=w)
                    return [ta, tb]

            def inproj_ftile(buf, wk, pm_extra):
                wkl = wk if isinstance(wk, list) else [wk]
                mm_w = wkl + [t_xt_done] + pm_extra
                t = None
                for k in range(NK):
                    t = mm(pm[buf][:, 0:TT], winT[buf][:, k * 128:(k + 1) * 128],
                           xtk[:, k, :], waits=mm_w if k == 0 else (),
                           inc=(k == NK - 1), start=(k == 0), stop=(k == NK - 1))
                return t

            t_win_used = [None, None]
            t_pm = [tok_pm[0], tok_pm[1]]
            conv_last = None
            t_dt_evac = None
            w_cwl = [t_cwcb]
            if tok_r5_free is not None:
                w_cwl.append(tok_r5_free)
            t_cwl = dma("sync", "cwcb", r5f[:, 0:400], cwcb_d, waits=w_cwl)
            for idx, f in enumerate(range(XBCF0, DTF + 1)):
                buf = idx % 2
                extra = []
                if idx <= 1:
                    if tok_r3_free is not None:
                        extra.append(tok_r3_free)
                    if pass1 and tok_r2_free is not None:
                        extra.append(tok_r2_free)
                    if pass1 and tt == 0:
                        extra.append(t_rowd)
                wk = acquire_win(f, buf, t_win_used[buf], extra=extra)
                t_g32 = inproj_ftile(buf, wk, [t_pm[buf]])
                t_win_used[buf] = t_g32
                if f == DTF:
                    t_dt_evac = op("vector", "tensor_copy", dtt, pm[buf][:, 0:TT],
                                   waits=[t_g32, conv_last])
                    t_pm[buf] = t_dt_evac
                else:
                    j = f - XBCF0
                    xtile = r0j[:, j, :]
                    w_ev = [t_g32]
                    if idx == 0 and tok_r0_free is not None:
                        w_ev.append(tok_r0_free)
                    t_ev = op("vector", "tensor_copy", xtile, pm[buf][:, 0:TT],
                              waits=w_ev)
                    tl = tailb[:, j * 3: j * 3 + 3]
                    w0 = cw_all[:, j * 4:j * 4 + 1]
                    w1 = cw_all[:, j * 4 + 1:j * 4 + 2]
                    w2c = cw_all[:, j * 4 + 2:j * 4 + 3]
                    w3 = cw_all[:, j * 4 + 3:j * 4 + 4]
                    t_c0 = op("scalar", "activation", out=convtmp, in_=xtile,
                              func=AF.Identity, bias=cb_col[:, j:j + 1], scale=w3,
                              waits=[t_ev, t_cwl])
                    t_c1 = op("vector", "scalar_tensor_tensor", convtmp[:, 1:TT],
                              xtile[:, 0:TT - 1], w2c, convtmp[:, 1:TT],
                              ALU.mult, ALU.add, waits=[t_c0])
                    t_c2 = op("vector", "scalar_tensor_tensor", convtmp[:, 2:TT],
                              xtile[:, 0:TT - 2], w1, convtmp[:, 2:TT],
                              ALU.mult, ALU.add, waits=[t_c1])
                    t_c3 = op("vector", "scalar_tensor_tensor", convtmp[:, 3:TT],
                              xtile[:, 0:TT - 3], w0, convtmp[:, 3:TT],
                              ALU.mult, ALU.add, waits=[t_c2])
                    t_b0 = op("vector", "scalar_tensor_tensor", convtmp[:, 0:3],
                              tl, w0, convtmp[:, 0:3], ALU.mult, ALU.add,
                              waits=[t_c3])
                    t_b1 = op("vector", "scalar_tensor_tensor", convtmp[:, 0:2],
                              tl[:, 1:3], w1, convtmp[:, 0:2], ALU.mult, ALU.add,
                              waits=[t_b0])
                    t_b2 = op("vector", "scalar_tensor_tensor", convtmp[:, 0:1],
                              tl[:, 2:3], w2c, convtmp[:, 0:1], ALU.mult, ALU.add,
                              waits=[t_b1])
                    t_tl2 = op("vector", "tensor_copy", tl, xtile[:, TT - 3:TT],
                               waits=[t_b2])
                    t_si = op("scalar", "activation", out=xtile, in_=convtmp,
                              func=AF.Silu, waits=[t_tl2])
                    t_pm[buf] = t_ev
                    conv_last = t_si
            t_wk_last = t_win_used[0]

            t_s0 = op("vector", "tensor_scalar_add", sp1x, dtt, dtbcol,
                      waits=[t_dt_evac, t_dtb])
            t_s1 = op("scalar", "activation", out=sp2x, in_=sp1x, func=AF.Abs,
                      waits=[t_s0])
            t_s2 = op("scalar", "activation", out=sp2x, in_=sp2x, func=AF.Exp,
                      scale=-1.0, waits=[t_s1])
            t_s3 = op("scalar", "activation", out=sp2x, in_=sp2x, func=AF.Ln, bias=1.0,
                      waits=[t_s2])
            t_s4 = op("vector", "tensor_single_scalar", sp1x, sp1x, 0.0, ALU.max,
                      waits=[t_s3])
            t_dtp = op("vector", "tensor_add", dtproc, sp1x, sp2x, waits=[t_s4])
            if _PH == "a" and last_tile:
                t_out_last = dma("scalar", "ob", out_h[0:128, :], obuf,
                                 waits=[t_dtp, conv_last])
                break

            w_cr = [t_rowd]
            if pass1:
                w_cr.append(t_win_used[0])
                w_cr.append(t_win_used[1])
            elif tok_r2_free is not None:
                w_cr.append(tok_r2_free)
            t_const = []
            for i in range(2):
                t_const.append(dma("sync", "cr", dhl[i][0:2, :],
                                   row_d[0:1, :].broadcast_to([2, 4096]),
                                   waits=w_cr + (t_const[-1:] or [])))
                t_const.append(dma("sync", "cr", rhs4[i][2:4, :],
                                   row_d[1:2, :].broadcast_to([2, 4096]),
                                   waits=[t_const[-1]]))

            t_y_evac = {}
            t_zwaits = []
            t_mp = [None, None]
            t_yev = [None, None]
            t_ones256 = op("gpsimd", "memset", onesf256, 1.0, waits=[t_xt_done])
            for cc in range(NCH):
                par = (tt * NCH + cc) % 2
                ccol = cc * CH
                t_da = op("vector", "tensor_scalar_mul", dab,
                          dtproc[:, ccol:ccol + CH], acol, waits=[t_dtp, t_aneg])
                t_acs = op("vector", "tensor_tensor_scan", acsb, dab, onesf256, 0.0,
                           ALU.add, ALU.mult, waits=[t_da, t_ones256])
                t_hi = op("vector", "tensor_copy", acshi, acsb, waits=[t_acs])
                t_lo32 = op("vector", "tensor_tensor", dsb, acsb, acshi,
                            ALU.subtract, waits=[t_hi])
                t_lo = op("vector", "tensor_copy", acslo, dsb, waits=[t_lo32])
                w_img = [t_refill_last[par]] if t_refill_last[par] else []
                t_wh = dma("scalar", "img%d" % par, acs_d[par, 0], acshi,
                           waits=[t_hi] + w_img)
                t_wl = dma("scalar", "img%d" % par, acs_d[par, 1], acslo,
                           waits=[t_lo, t_wh] + w_img)
                t_ds = op("scalar", "activation", out=dsb, in_=acsb, func=AF.Exp,
                          scale=-1.0, bias=acsb[:, CH - 1:CH], waits=[t_lo32])
                t_dtds = op("vector", "tensor_tensor", dtdsb, dsb,
                            dtproc[:, ccol:ccol + CH], ALU.mult, waits=[t_ds])
                t_cd0 = op("scalar", "activation", out=smallf[:, 32:33],
                           in_=acsb[:, CH - 1:CH], func=AF.Exp, waits=[t_acs])
                t_cdt = tr(pt[0][0:1, 0:128], smallf[:, 32:33],
                           waits=[t_cd0] + t_zwaits)
                t_cdr = op("vector", "tensor_copy", cdrow, pt[0][0:1, 0:128],
                           waits=[t_cdt])
                t_cdbm = mm(pm[0][:, 0:128], onesf1, cdrow,
                            waits=[t_cdr, t_pm[0]])
                t_cdb = op("vector", "tensor_copy", cdb, pm[0][:, 0:128],
                           waits=[t_cdbm])
                t_pm[0] = t_cdb
                tk1 = tr(pt[1][:, 0:128], dtproc[:, ccol:ccol + 128],
                         waits=[t_dtp] + t_zwaits)
                tk2 = tr(pt[1][:, 128:256], dtproc[:, ccol + 128:ccol + 256])
                t_dttT = op("vector", "tensor_copy", dttT, pt[1][:, 0:256],
                            waits=[tk2])
                tk3 = tr(pt[1][:, 256:384], dtdsb[:, 0:128],
                         waits=[t_dtds, t_dttT])
                tk4 = tr(pt[1][:, 384:512], dtdsb[:, 128:256])
                t_ddtT = op("vector", "tensor_copy", dtdsT, pt[1][:, 256:512],
                            waits=[tk4])
                t_gg = []
                t_btm_ev = None
                for g in range(NG):
                    bt = r0j[:, 64 + g, ccol:ccol + CH]
                    ct = r0j[:, 72 + g, ccol:ccol + CH]
                    w = []
                    if g == 0:
                        w = [conv_last, t_pm[1]]
                    if t_gg:
                        w = [t_gg[-1]]
                    mm(pm[1][:, 0:256], bt[:, 0:128], ct, waits=w, inc=False)
                    tkg = mm(pm[1][:, 256:384], bt[:, 128:256], ct[:, 128:256],
                             skip=True)
                    w_ge = [tkg]
                    if tt == 0 and cc == 0 and g == 0:
                        w_ge.append(t_aneg)
                    if tt == 0 and cc == 0 and g < 3:
                        w_ge.append(t_prep_done)
                    t_gg.append(op("vector", "tensor_copy", gg[g], pm[1][:, 0:384],
                                   waits=w_ge))
                    wbt = [t_ddtT] if g == 0 else [t_btm_ev]
                    b1 = tr(ptb[1][:, 0:128], bt[:, 0:128], waits=wbt)
                    b2 = tr(ptb[1][:, 128:256], bt[:, 128:256])
                    t_btm_ev = op("vector", "tensor_copy", btm[g],
                                  ptb[1][:, 0:256], waits=[b2])
                t_pm[1] = t_gg[-1]
                t_stage = [None] * 8

                def refill(b):
                    sb = b % 2
                    src = acs_d[par].rearrange("a h c -> a (h c)")[:, b * 4096:(b + 1) * 4096]
                    w = [t_wh, t_wl] + list(t_const)
                    if t_batch_mm[b - 2] is not None:
                        w.append(t_batch_mm[b - 2])
                    t1 = dma("sync", "st%d" % sb, dhl[sb][2:4, :], src, waits=w)
                    t2 = dma("sync", "st%d" % sb, rhs4[sb][0:2, :], src,
                             waits=[t1])
                    t_stage[b] = t2
                    return t2

                t_refill_last[par] = refill(0)

                t_actL = [None, None]
                t_actO = [None, None]
                t_mp = [None, None]
                t_mx = [None, None]
                t_dsx = [None, None]
                t_ymm = [None, None]
                t_yev = [None, None]
                t_sup = [None, None]
                t_supd_cur = [None] * 64
                pend = {}
                for it in range(65):
                  if it < 64:
                    pr = it
                    pp = pr % 2
                    hA, hB = 2 * pr, 2 * pr + 1
                    g = hA // 16
                    if pr % 8 == 0 and pr // 8 + 1 < 8:
                        t_refill_last[par] = refill(pr // 8 + 1)
                    xcp = r0j[:, pr, ccol:ccol + CH]
                    w = []
                    if t_dsx[pp] is not None:
                        w.append(t_dsx[pp])
                    if pr == 0:
                        w += [t_btm_ev, t_dttT, t_ddtT]
                    xt1 = tr(ptb[pp][:, 0:128], xcp[:, 0:128], waits=w)
                    xt2 = tr(ptb[pp][:, 128:256], xcp[:, 128:256])
                    wx = [xt2, t_dttT]
                    if t_ymm[pp] is not None:
                        wx.append(t_ymm[pp])
                    if pr <= 1 and tok_r3_free is not None and tt + cc == 0:
                        wx.append(tok_r3_free)
                    if pr == 0 and cc == 0 and pass1:
                        wx += [t for t in t_cache_win if t is not None]
                    t_x1 = op("vector", "tensor_tensor",
                              xdtb[pp][:, 0:128].rearrange("p (h x) -> p h x", h=2),
                              ptb[pp][:, 0:128].rearrange("p (h x) -> p h x", h=2),
                              dttT[:, 2 * pr:2 * pr + 2].unsqueeze(2).broadcast_to([128, 2, 64]),
                              ALU.mult, waits=wx)
                    t_x2 = op("vector", "tensor_tensor",
                              xdtb[pp][:, 128:256].rearrange("p (h x) -> p h x", h=2),
                              ptb[pp][:, 128:256].rearrange("p (h x) -> p h x", h=2),
                              dttT[:, 128 + 2 * pr:128 + 2 * pr + 2].unsqueeze(2).broadcast_to([128, 2, 64]),
                              ALU.mult, waits=[t_x1])
                    t_d1 = op("vector", "tensor_tensor",
                              dsxdtb[pp][:, 0:128].rearrange("p (h x) -> p h x", h=2),
                              ptb[pp][:, 0:128].rearrange("p (h x) -> p h x", h=2),
                              dtdsT[:, 2 * pr:2 * pr + 2].unsqueeze(2).broadcast_to([128, 2, 64]),
                              ALU.mult, waits=[t_x2, t_ddtT])
                    t_d2 = op("vector", "tensor_tensor",
                              dsxdtb[pp][:, 128:256].rearrange("p (h x) -> p h x", h=2),
                              ptb[pp][:, 128:256].rearrange("p (h x) -> p h x", h=2),
                              dtdsT[:, 128 + 2 * pr:128 + 2 * pr + 2].unsqueeze(2).broadcast_to([128, 2, 64]),
                              ALU.mult, waits=[t_d1])
                    t_dsx[pp] = t_d2

                    t_mp_pair = []
                    t_mx_pair = []
                    for h in (hA, hB):
                        hp = h % 2
                        batch = h // 16
                        sb = batch % 2
                        bi = h % 16
                        w = [t_stage[batch]]
                        if t_actL[hp] is not None:
                            w.append(t_actL[hp])
                        mm(pdk[hp][:, 0:256], dhl[sb][:, bi * 256:bi * 256 + 128],
                           rhs4[sb][:, bi * 256:bi * 256 + 256],
                           waits=w, inc=False, start=True, stop=False)
                        mm(pdk[hp][:, 0:128], identb, trimask, waits=[tok_setup],
                           inc=False, start=False, stop=True)
                        mm(pdk[hp][:, 256:384],
                           dhl[sb][:, bi * 256 + 128:bi * 256 + 256],
                           rhs4[sb][:, bi * 256 + 128:bi * 256 + 256],
                           inc=False, start=True, stop=False, skip=True)
                        t_dmm = mm(pdk[hp][:, 256:384], identb, trimask,
                                   start=False, stop=True, skip=True)
                        w2 = []
                        if pr == 0:
                            w2.append(t_pm[hp])
                        if t_actO[hp] is not None:
                            w2.append(t_actO[hp])
                        t_omm = mm(pm[hp][:, 0:256],
                                   dhl[sb][0:2, 0:128],
                                   rhs4[sb][0:2, bi * 256:bi * 256 + 256], waits=w2)
                        t_batch_mm[batch] = t_omm
                        wL = [t_dmm]
                        if t_mp[hp] is not None:
                            wL.append(t_mp[hp])
                        t_L = op("scalar", "activation", out=lbuf[hp],
                                 in_=pdk[hp][:, 0:384], func=AF.Exp, waits=wL)
                        t_actL[hp] = t_L
                        wO = [t_omm]
                        if t_mx[hp] is not None:
                            wO.append(t_mx[hp])
                        t_O = op("scalar", "activation", out=odbb[hp],
                                 in_=pm[hp][:, 0:256], func=AF.Exp, waits=wO)
                        t_actO[hp] = t_O
                        wM = [t_L, t_gg[g]]
                        if t_ymm[pp] is not None:
                            wM.append(t_ymm[pp])
                        t_M = op("gpsimd", "tensor_tensor", mpb2[pp][hp],
                                 lbuf[hp], gg[g], ALU.mult, waits=wM)
                        t_mp[hp] = t_M
                        ct = r0j[:, 72 + g, ccol:ccol + CH]
                        wXx = [t_O]
                        if pr == 0 and cc == 0 and pass1:
                            wXx += [t for t in t_cache_win if t is not None]
                        t_X = op("gpsimd", "tensor_tensor", mxb2[pp][hp],
                                 odbb[hp], ct, ALU.mult, waits=wXx)
                        t_mx[hp] = t_X
                        t_mp_pair.append(t_M)
                        t_mx_pair.append(t_X)
                    pend[pr] = (t_d2, t_mp_pair, t_mx_pair)

                  if it >= 1:
                    pr = it - 1
                    pp = pr % 2
                    hA, hB = 2 * pr, 2 * pr + 1
                    g = hA // 16
                    xcp = r0j[:, pr, ccol:ccol + CH]
                    t_d2p, t_mpp, t_mxp = pend.pop(pr)
                    wY = [t_d2p] + t_mpp + t_mxp
                    if t_yev[pp] is not None:
                        wY.append(t_yev[pp])
                    if t_sup[pp] is not None:
                        wY.append(t_sup[pp])
                    if t_supd_prev[pr] is not None:
                        wY.append(t_supd_prev[pr])
                    mm(py[pp][0:64, 0:256], xdtb[pp][:, 0:64],
                       mpb2[pp][0][:, 0:256],
                       waits=wY, inc=False, start=True, stop=False, tp=(0, 0),
                       skip=True)
                    mm(py[pp][0:64, 128:256], xdtb[pp][:, 128:192],
                       mpb2[pp][0][:, 256:384], inc=False, start=False,
                       stop=False, tp=(0, 0), skip=True)
                    mm(py[pp][0:64, 0:256], r4[:, hA * 64:(hA + 1) * 64],
                       mxb2[pp][0],
                       inc=False, start=False, stop=True, tp=(0, 0), skip=True)
                    mm(py[pp][64:128, 0:256], xdtb[pp][:, 64:128],
                       mpb2[pp][1][:, 0:256],
                       inc=False, start=True, stop=False, tp=(0, 64), skip=True)
                    mm(py[pp][64:128, 128:256], xdtb[pp][:, 192:256],
                       mpb2[pp][1][:, 256:384], inc=False, start=False,
                       stop=False, tp=(0, 64), skip=True)
                    mm(py[pp][64:128, 0:256], r4[:, hB * 64:(hB + 1) * 64],
                       mxb2[pp][1],
                       inc=False, start=False, stop=True, tp=(0, 64), skip=True)
                    mm(py[pp][:, 256:384], btm[g][:, 0:128], dsxdtb[pp][:, 0:128],
                       inc=False, start=True, stop=False, skip=True)
                    t_ymm[pp] = mm(py[pp][:, 256:384], btm[g][:, 128:256],
                                   dsxdtb[pp][:, 128:256], start=False, stop=True,
                                   skip=True)
                    t_ye = op("vector", "scalar_tensor_tensor", xcp, xcp,
                              dpairs[:, pr:pr + 1], py[pp][:, 0:256],
                              ALU.mult, ALU.add, waits=[t_ymm[pp], t_dp2])
                    t_yev[pp] = t_ye
                    t_y_evac[(cc, pr)] = t_ye
                    t_sa = op("vector", "scalar_tensor_tensor",
                              r4[:, hA * 64:(hA + 1) * 64],
                              r4[:, hA * 64:(hA + 1) * 64],
                              cdb[:, hA:hA + 1], py[pp][:, 256:320],
                              ALU.mult, ALU.add, waits=[t_ymm[pp], t_cdb])
                    t_sb = op("vector", "scalar_tensor_tensor",
                              r4[:, hB * 64:(hB + 1) * 64],
                              r4[:, hB * 64:(hB + 1) * 64],
                              cdb[:, hB:hB + 1], py[pp][:, 320:384],
                              ALU.mult, ALU.add, waits=[t_sa])
                    t_sup[pp] = t_sb
                    t_supd_cur[pr] = t_sb
                t_supd_prev = t_supd_cur
                t_pm = [t_actO[0], t_actO[1]]
                t_zwaits = [t_ymm[0], t_ymm[1]]
            if _PH == "b" and last_tile:
                t_out_last = dma("scalar", "ob", out_h[0:128, :], obuf,
                                 waits=t_zwaits + [t_yev[0], t_yev[1], t_sup[0], t_sup[1]])
                break

            t_win_used = [None, None]
            t_rms_mm = None
            t_rmsp = [None, None]
            t_gate_last = None
            t_zs = [None, None]
            pend_z = {}
            for itz in range(NFD + 1):
              if itz < NFD:
                f = itz
                buf = f % 2
                extra = list(t_zwaits) if f <= 1 else []
                wk = acquire_win(f, buf, t_win_used[buf], extra=extra)
                pm_extra = [t_pm[buf]]
                if f == 0:
                    pm_extra += [t for t in t_yev if t is not None]
                t_g32 = inproj_ftile(buf, wk, pm_extra)
                t_win_used[buf] = t_g32
                zs = opfm2[f % 2][:, 0:TT]
                w_zs = [t_g32]
                if t_zs[f % 2] is not None:
                    w_zs.append(t_zs[f % 2])
                t_z = op("scalar", "activation", out=zs, in_=pm[buf][:, 0:TT],
                         func=AF.Silu, waits=w_zs)
                t_pm[buf] = t_z
                pend_z[f] = t_z
              if itz >= 1:
                f = itz - 1
                t_z = pend_z.pop(f)
                zs = opfm2[f % 2][:, 0:TT]
                ytile = r0j[:, f, :]
                wg = [t_z]
                for cc2 in range(NCH):
                    if (cc2, f) in t_y_evac:
                        wg.append(t_y_evac[(cc2, f)])
                t_g = op("vector", "tensor_tensor", ytile, ytile, zs, ALU.mult,
                         waits=wg)
                t_zs[f % 2] = t_g
                w_g2 = [t_g]
                if t_rmsp[f % 2] is not None:
                    w_g2.append(t_rmsp[f % 2])
                t_g2 = op("vector", "tensor_tensor", g2b2[f % 2], ytile, ytile,
                          ALU.mult, waits=w_g2)
                w_rmm = [t_g2]
                if f == 0:
                    w_rmm.append(t_cdr)
                t_rms_mm = mm(pt[0][0:1, 0:TT], onescol, g2b2[f % 2],
                              waits=w_rmm, start=(f == 0), stop=(f == NFD - 1))
                t_rmsp[f % 2] = t_rms_mm
                t_gate_last = t_g
            t_zmm_last = t_win_used[1]
            t_rrow = op("vector", "tensor_copy", rmsrow, pt[0][0:1, 0:TT],
                        waits=[t_rms_mm])
            t_ln1 = op("scalar", "activation", out=rmsrow, in_=rmsrow, func=AF.Ln,
                       scale=1.0 / DI, bias=EPS, waits=[t_rrow])
            t_ex1 = op("scalar", "activation", out=rmsrow, in_=rmsrow, func=AF.Exp,
                       scale=-0.5, waits=[t_ln1])
            t_it = None
            for ts in range(4):
                t_it = tr(pt[0][:, 256 + ts:257 + ts],
                          rmsrow[:, ts * 128:(ts + 1) * 128],
                          waits=[t_ex1] if ts == 0 else ())
            t_inv = op("vector", "tensor_copy", invc, pt[0][:, 256:260],
                       waits=[t_it])
            if _PH == "z" and last_tile:
                t_out_last = dma("scalar", "ob", out_h[0:128, :], obuf,
                                 waits=[t_inv, t_gate_last])
                break

            def acquire_wout(dmt, buf, wprev, extra=()):
                if pass1:
                    evg = []
                    grp = 0
                    last = None
                    lastv = None
                    lasts = None
                    for q in range(4):
                        w = list(extra) + ([wprev] if wprev and q == 0 else [])
                        if wout_qprev[q % 2] is not None:
                            w.append(wout_qprev[q % 2])
                        tn = dma("sync", "wq%d" % (q % 2), wnatq[q % 2],
                                 w_out[dmt * 128:(dmt + 1) * 128,
                                       q * 2048:(q + 1) * 2048], waits=w)
                        if _OSUB == "dma":
                            wout_qprev[q % 2] = tn
                            last = tn
                            continue
                        for fg in range(4):
                            tts = []
                            for f4 in range(4):
                                fs = fg * 4 + f4
                                w2 = [tn]
                                if len(evg) >= 2 and f4 == 0:
                                    w2.append(evg[-2])
                                tts.append(tr(
                                    pdk[grp % 2][:, f4 * 128:(f4 + 1) * 128],
                                    wnatq[q % 2][:, fs * 128:(fs + 1) * 128],
                                    waits=w2))
                            if _OSUB == "tr":
                                ev0 = op("vector", "tensor_copy",
                                         opfm[:, 0:512], pdk[grp % 2][:, 0:512],
                                         waits=[tts[-1]] + ([evg[-1]] if evg else []))
                                evg.append(ev0)
                                last = ev0
                                grp += 1
                                wout_qprev[q % 2] = ev0
                                continue
                            evs4 = []
                            geng = "vector" if grp % 2 == 0 else "scalar"
                            for f4 in range(4):
                                ft = q * 16 + fg * 4 + f4
                                ew = [tts[-1]]
                                if grp <= 1 and f4 == 0 and t_cache_wout[buf] is not None:
                                    ew.append(t_cache_wout[buf])
                                if geng == "vector":
                                    ev = op("vector", "tensor_scalar",
                                            woasm[buf][:, ft * 128:(ft + 1) * 128],
                                            pdk[grp % 2][:, f4 * 128:(f4 + 1) * 128],
                                            nw_col[:, ft:ft + 1], None, ALU.mult,
                                            waits=ew)
                                else:
                                    ev = op("scalar", "activation",
                                            out=woasm[buf][:, ft * 128:(ft + 1) * 128],
                                            in_=pdk[grp % 2][:, f4 * 128:(f4 + 1) * 128],
                                            func=AF.Identity,
                                            scale=nw_col[:, ft:ft + 1],
                                            waits=ew)
                                evs4.append(ev)
                            last = evs4[-1]
                            if geng == "vector":
                                lastv = last
                            else:
                                lasts = last
                            evg.append(last)
                            grp += 1
                        wout_qprev[q % 2] = evg[-1]
                    if _OSUB not in ("dma", "tr", "ev"):
                        if lastv is not None and lasts is not None:
                            last = op("vector", "engine_nop",
                                      waits=[lastv, lasts])
                        t_cache_wout[buf] = dma("sync", "cwout%d" % buf,
                                                wout_cache[dmt], woasm[buf],
                                                waits=[last])
                    return last
                else:
                    w = list(extra) + ([wprev] if wprev else [])
                    ta = dma("sync", "wo%d" % buf, woasm[buf][:, 0:4096],
                             wout_cache[dmt][:, 0:4096], waits=w)
                    tb = dma("sync", "wo%d" % (buf + 2),
                             woasm[buf][:, 4096:8192],
                             wout_cache[dmt][:, 4096:8192], waits=w)
                    return [ta, tb]

            t_wo_used = [None, None]
            t_opfm_free = [None, None]
            wout_qprev = [None, None]
            t_pt_free = [t_inv, None]
            t_lnm_done = [[] for _ in range(4)]
            t_last_opmm = None
            n_dmt = min(NK, _ODM)
            pend_o = {}
            for it_o in range(n_dmt + 1):
              if it_o < n_dmt:
                dmt = it_o
                buf = dmt % 2
                extra = []
                if dmt <= 1:
                    extra += t_zwaits
                    if pass1:
                        extra.append(t_zmm_last)
                        extra += [t for t in t_cache_win if t is not None]
                wk = acquire_wout(dmt, buf, t_wo_used[buf], extra=extra)
                wkl = wk if isinstance(wk, list) else [wk]
                if _OSUB == "acq":
                    t_wo_used[buf] = wkl[-1]
                    t_lnm_done = None
                    continue
                mm_w = wkl + [t_gate_last, t_pm[buf]]
                t_mmo = None
                for f in range(NFD):
                    t_mmo = mm(pm[buf][:, 0:TT],
                               woasm[buf][:, f * 128:(f + 1) * 128], r0j[:, f, :],
                               waits=mm_w if f == 0 else (), inc=(f == NFD - 1),
                               start=(f == 0), stop=(f == NFD - 1))
                t_wo_used[buf] = t_mmo
                t_last_opmm = t_mmo
                if _OSUB == "mm":
                    t_pm[buf] = op("vector", "tensor_copy",
                                   opfm[:, 0:TT], pm[buf][:, 0:TT],
                                   waits=[t_mmo] + ([t_pm[buf]] if t_pm[buf] else []))
                    continue
                w_ec = [t_mmo]
                if t_opfm_free[dmt % 2] is not None:
                    w_ec.append(t_opfm_free[dmt % 2])
                t_ec = op("vector", "tensor_copy", opfm2[dmt % 2][:, 0:TT],
                          pm[buf][:, 0:TT], waits=w_ec)
                t_pm[buf] = t_ec
                pend_o[dmt] = t_ec
              if it_o >= 1 and _OSUB not in ("acq", "mm"):
                dmt = it_o - 1
                t_ec = pend_o.pop(dmt)
                ptx = dmt % 2
                trs = None
                for ts in range(4):
                    w_tr = []
                    if ts == 0:
                        w_tr = [t_ec]
                        if t_pt_free[ptx] is not None:
                            w_tr.append(t_pt_free[ptx])
                    trs = tr(ptb[ptx][:, ts * 128:(ts + 1) * 128],
                             opfm2[dmt % 2][:, ts * 128:(ts + 1) * 128],
                             waits=w_tr)
                evl = None
                beng = "vector" if dmt % 2 == 0 else "scalar"
                for ts in range(4):
                    dst = lnm[ts][:, dmt * 128:(dmt + 1) * 128]
                    w_e = [trs, t_inv]
                    if dmt == 0 and ts == 0:
                        w_e.append(t_zmm_last)
                    if beng == "vector":
                        evl = op("vector", "tensor_scalar", dst,
                                 ptb[ptx][:, ts * 128:(ts + 1) * 128],
                                 invc[:, ts:ts + 1], None, ALU.mult, waits=w_e)
                    else:
                        evl = op("scalar", "activation", out=dst,
                                 in_=ptb[ptx][:, ts * 128:(ts + 1) * 128],
                                 func=AF.Identity, scale=invc[:, ts:ts + 1],
                                 waits=w_e)
                    t_lnm_done[ts].append(evl)
                t_pt_free[ptx] = evl
                t_opfm_free[dmt % 2] = evl
            if _PH == "o" and last_tile:
                if _OSUB == "acq":
                    wl = [t_wo_used[0] or t_inv, t_wo_used[1] or t_inv]
                elif _OSUB == "mm":
                    wl = [t_pm[0], t_pm[1]]
                else:
                    wl = [t_lnm_done[3][-1], t_lnm_done[0][-1]]
                t_out_last = dma("scalar", "ob", out_h[0:128, :], obuf,
                                 waits=wl)
                break

            t_x1 = []
            t_prev_r = None
            for ts in range(4):
                w = list(t_lnm_done[ts])
                if t_prev_obuf is not None:
                    w.append(t_prev_obuf)
                if t_prev_r is not None:
                    w.append(t_prev_r)
                if ts == 0:
                    w += [t_mp[0], t_mp[1], t_gate_last]
                tn = dma("scalar", "xr", xres,
                         hs[c0 + ts * 128: c0 + (ts + 1) * 128, :], waits=w)
                t_r = op("vector", "scalar_tensor_tensor", lnm[ts], xres, 2.0,
                         lnm[ts], ALU.mult, ALU.add,
                         waits=[tn] + list(t_lnm_done[ts]))
                t_st1 = op("scalar", "activation", out=ln2b, in_=lnm[ts],
                           func=AF.Identity, accum_out=smallf[:, 0:1], waits=[t_r])
                t_st2 = op("scalar", "activation", out=ln2b, in_=lnm[ts],
                           func=AF.Square, accum_out=smallf[:, 1:2], waits=[t_st1])
                t_mu = op("vector", "tensor_scalar_mul", smallf[:, 2:3],
                          smallf[:, 0:1], 1.0 / DM, waits=[t_st2])
                t_m2 = op("vector", "tensor_scalar_mul", smallf[:, 3:4],
                          smallf[:, 1:2], 1.0 / DM, waits=[t_mu])
                t_nm = op("vector", "tensor_scalar_mul", smallf[:, 4:5],
                          smallf[:, 2:3], -1.0, waits=[t_m2])
                t_va = op("vector", "scalar_tensor_tensor", smallf[:, 5:6],
                          smallf[:, 2:3], smallf[:, 4:5], smallf[:, 3:4],
                          ALU.mult, ALU.add, waits=[t_nm])
                t_l1 = op("scalar", "activation", out=smallf[:, 6:7],
                          in_=smallf[:, 5:6], func=AF.Ln, bias=EPS, waits=[t_va])
                t_iv = op("scalar", "activation", out=smallf[:, 6:7],
                          in_=smallf[:, 6:7], func=AF.Exp, scale=-0.5, waits=[t_l1])
                t_n2 = op("vector", "tensor_tensor", smallf[:, 7:8],
                          smallf[:, 2:3], smallf[:, 6:7], ALU.mult, waits=[t_iv])
                t_n3 = op("vector", "tensor_scalar_mul", smallf[:, 7:8],
                          smallf[:, 7:8], -1.0, waits=[t_n2])
                t_x1t = op("vector", "tensor_scalar", lnm[ts], lnm[ts],
                           smallf[:, 6:7], smallf[:, 7:8], ALU.mult, ALU.add,
                           waits=[t_n3])
                t_x1.append(t_x1t)
                t_prev_r = t_r
            t_x1fm_ev = []
            for ts in range(4):
                for kg in range(8):
                    tts = []
                    for k4 in range(4):
                        k = kg * 4 + k4
                        w2 = []
                        if k4 == 0:
                            w2 = [t_x1[ts]]
                            if ts == 0 and kg == 0:
                                w2 += [t_last_opmm, t_pt_free[0], t_pt_free[1]]
                                if pass1:
                                    w2 += [t for t in t_cache_wout if t is not None]
                            if len(t_x1fm_ev) >= 2:
                                w2.append(t_x1fm_ev[-2])
                        tts.append(tr(ptb[kg % 2][:, k4 * 128:(k4 + 1) * 128],
                                      lnm[ts][:, k * 128:(k + 1) * 128], waits=w2))
                    t_x1fm_ev.append(op(
                        "vector", "tensor_copy",
                        x1fmk[:, kg * 4:(kg + 1) * 4, ts * 128:(ts + 1) * 128],
                        ptb[kg % 2].rearrange("p (k c) -> p k c", c=128)[:, 0:4, :],
                        waits=tts))
            t_x1fm = t_x1fm_ev[-1]

            t_fl = dma("sync", "ffn", acat, ffn_d[:, 0:2048],
                       waits=[t_last_opmm, t_zmm_last])
            t_fl = dma("sync", "ffn", bcat, ffn_d[0:64, 2048:6144], waits=[t_fl])
            t_fl = dma("sync", "ffn", wrt, ffn_d[:, 6144:6272], waits=[t_fl])
            t_rmm = None
            for k in range(NK):
                t_rmm = mm(pm[0][0:4, 0:TT], wrt[:, k * 4:(k + 1) * 4],
                           x1fmk[:, k, :],
                           waits=[t_x1fm, t_pm[0], t_fl] if k == 0 else (),
                           inc=(k == NK - 1), start=(k == 0), stop=(k == NK - 1))
            t_rev = op("vector", "tensor_copy", rraw, pm[0][0:4, 0:TT],
                       waits=[t_rmm])
            t_pm[0] = t_rev
            t_lt = None
            for ts in range(4):
                t_lt = tr(pt[0][:, 320 + ts * 4:320 + ts * 4 + 4],
                          rraw[0:4, ts * 128:(ts + 1) * 128],
                          waits=[t_rev, t_x1fm_ev[-2]] if ts == 0 else ())
            t_log = op("vector", "tensor_copy", logits, pt[0][:, 320:336],
                       waits=[t_lt])
            t_pt0_free = t_log
            t_gts = None
            for ts in range(4):
                lg = logits[:, ts * 4:(ts + 1) * 4]
                t_m1 = op("vector", "reduce_max", smallf[:, 16:17], lg, AX.X,
                          waits=[t_log] if ts == 0 else [t_gts])
                t_mn = op("vector", "tensor_scalar_mul", smallf[:, 17:18],
                          smallf[:, 16:17], -1.0, waits=[t_m1])
                t_e = op("scalar", "activation", out=gatesb[:, ts * 4:(ts + 1) * 4],
                         in_=lg, func=AF.Exp, bias=smallf[:, 17:18], waits=[t_mn])
                t_cn = None
                for e in range(4):
                    bce = lg[:, e:e + 1].broadcast_to([128, 4])
                    t_gt = op("vector", "tensor_tensor", cmp4[:, 0:4], lg, bce,
                              ALU.is_gt, waits=[t_e] if e == 0 else [t_cn])
                    t_eq = op("vector", "tensor_tensor", cmp4[:, 4:8], lg, bce,
                              ALU.is_equal, waits=[t_gt])
                    t_ej = op("vector", "tensor_tensor", cmp4[:, 4:8], cmp4[:, 4:8],
                              jlt[:, e * 4:(e + 1) * 4], ALU.mult, waits=[t_eq])
                    t_ad = op("vector", "tensor_tensor", cmp4[:, 0:4], cmp4[:, 0:4],
                              cmp4[:, 4:8], ALU.add, waits=[t_ej])
                    t_cn = op("vector", "reduce_sum", cnt4[:, e:e + 1],
                              cmp4[:, 0:4], AX.X, waits=[t_ad])
                t_mk = op("vector", "tensor_single_scalar", cnt4, cnt4, 2.0,
                          ALU.is_lt, waits=[t_cn])
                t_em = op("vector", "tensor_tensor", gatesb[:, ts * 4:(ts + 1) * 4],
                          gatesb[:, ts * 4:(ts + 1) * 4], cnt4, ALU.mult,
                          waits=[t_mk])
                t_sm = op("vector", "reduce_sum", smallf[:, 18:19],
                          gatesb[:, ts * 4:(ts + 1) * 4], AX.X, waits=[t_em])
                t_rc = op("vector", "reciprocal", smallf[:, 19:20],
                          smallf[:, 18:19], waits=[t_sm])
                t_gts = op("vector", "tensor_scalar_mul",
                           gatesb[:, ts * 4:(ts + 1) * 4],
                           gatesb[:, ts * 4:(ts + 1) * 4], smallf[:, 19:20],
                           waits=[t_rc])
            t_gT = None
            for ts in range(4):
                t_gT = tr(pt[0][0:4, ts * 128:(ts + 1) * 128],
                          gatesb[:, ts * 4:(ts + 1) * 4],
                          waits=[t_gts, t_pt0_free] if ts == 0 else ())
            t_gTe = op("vector", "tensor_copy", gatesT, pt[0][0:4, 0:512],
                       waits=[t_gT])
            t_gTb = op("vector", "tensor_copy", gatesTb, gatesT, waits=[t_gTe])
            t_grep = mm(py[0][0:64, 0:TT], e4, gatesTb,
                        waits=[t_gTb, t_sup[0], t_yev[0]])
            t_mmm = None
            for k in range(NK):
                t_mmm = mm(pm[1][0:64, 0:TT], acat[:, k * 64:(k + 1) * 64],
                           x1fmk[:, k, :],
                           waits=[t_x1fm, t_pm[1], t_fl] if k == 0 else (),
                           inc=(k == NK - 1), start=(k == 0), stop=(k == NK - 1))
            t_me = op("vector", "tensor_copy", midb, pm[1][0:64, 0:TT],
                      waits=[t_mmm])
            t_pm[1] = t_me
            t_gm = op("vector", "tensor_tensor", gmidb, midb, py[0][0:64, 0:TT],
                      ALU.mult, waits=[t_me, t_grep])
            t_bankfree = [t_sup[1], None]
            t_ln2_add = None
            for ts in range(4):
                t_adds = []
                for obk in range(8):
                    bank = py[1] if obk % 2 == 0 else pdk[1]
                    w = [t_gm]
                    if t_bankfree[obk % 2] is not None:
                        w.append(t_bankfree[obk % 2])
                    tmm2 = mm(bank[:, 0:TT], gmidb[:, ts * 128:(ts + 1) * 128],
                              bcat[:, obk * 512:(obk + 1) * 512], waits=w)
                    t_last_deltamm = tmm2
                    w_ad = [tmm2]
                    if ts == 0 and obk == 0:
                        w_ad.append(t_st2 if False else t_x1[3])
                    tadd = op("vector", "tensor_tensor",
                              ln2b[:, obk * 512:(obk + 1) * 512], bank[:, 0:TT],
                              lnm[ts][:, obk * 512:(obk + 1) * 512], ALU.add,
                              waits=w_ad)
                    t_bankfree[obk % 2] = tadd
                    t_adds.append(tadd)
                w_s = t_adds
                if t_prev_obuf is not None:
                    w_s = t_adds + [t_prev_obuf]
                t_u1 = op("scalar", "activation", out=r0[:, 0:4096], in_=ln2b,
                          func=AF.Identity, accum_out=smallf[:, 20:21], waits=w_s)
                t_u2 = op("scalar", "activation", out=r0[:, 0:4096], in_=ln2b,
                          func=AF.Square, accum_out=smallf[:, 21:22], waits=[t_u1])
                t_mu = op("vector", "tensor_scalar_mul", smallf[:, 22:23],
                          smallf[:, 20:21], 1.0 / DM, waits=[t_u2])
                t_m2 = op("vector", "tensor_scalar_mul", smallf[:, 23:24],
                          smallf[:, 21:22], 1.0 / DM, waits=[t_mu])
                t_nm = op("vector", "tensor_scalar_mul", smallf[:, 24:25],
                          smallf[:, 22:23], -1.0, waits=[t_m2])
                t_va = op("vector", "scalar_tensor_tensor", smallf[:, 25:26],
                          smallf[:, 22:23], smallf[:, 24:25], smallf[:, 23:24],
                          ALU.mult, ALU.add, waits=[t_nm])
                t_l2 = op("scalar", "activation", out=smallf[:, 26:27],
                          in_=smallf[:, 25:26], func=AF.Ln, bias=EPS, waits=[t_va])
                t_i2 = op("scalar", "activation", out=smallf[:, 26:27],
                          in_=smallf[:, 26:27], func=AF.Exp, scale=-0.5,
                          waits=[t_l2])
                t_p2 = op("vector", "tensor_tensor", smallf[:, 27:28],
                          smallf[:, 22:23], smallf[:, 26:27], ALU.mult,
                          waits=[t_i2])
                t_p3 = op("vector", "tensor_scalar_mul", smallf[:, 27:28],
                          smallf[:, 27:28], -1.0, waits=[t_p2])
                t_x2 = op("vector", "tensor_scalar", obuf, ln2b,
                          smallf[:, 26:27], smallf[:, 27:28], ALU.mult, ALU.add,
                          waits=[t_p3])
                t_od = dma("scalar", "ob",
                           out_h[c0 + ts * 128: c0 + (ts + 1) * 128, :],
                           obuf, waits=[t_x2])
                t_prev_obuf = t_od
                t_out_last = t_od

            tok_r0_free = t_u2
            tok_r1_free = t_x1fm
            tok_r5_free = t_x2
            tok_r2_free = t_mmm
            tok_r3_free = t_last_deltamm
            tok_r6_free = t_out_last
            tok_pm = [t_pm[0], t_pm[1]]

        def f_fin(eng):
            return eng.nop()
        P.add("scalar", f_fin, waits=[t_out_last])

        for s, v in P.cnt.items():
            assert v < 60000, f"semaphore {s} overflows: {v}"

        def replay(name):
            def body(eng):
                waited = {}
                for (fn, waits, incinfo) in P.steps[name]:
                    for (s, v) in waits:
                        if waited.get(s, 0) < v:
                            eng.wait_ge(sems[s], v)
                            waited[s] = v
                    ins = fn(eng)
                    if incinfo is not None:
                        s, n = incinfo
                        ins.then_inc(sems[s], n)
            return body

        blk.sync(replay("sync"))
        blk.gpsimd(replay("gpsimd"))
        blk.scalar(replay("scalar"))
        blk.vector(replay("vector"))
        blk.tensor(replay("tensor"))

    return nc


def kernel(**inputs):
    hs_full = np.ascontiguousarray(inputs["L0_attn_hidden_states"], dtype=np.float32)
    nc = build_program()
    wkeys = ["L0_attn_in_proj_w", "L0_attn_conv_w", "L0_attn_conv_b",
             "L0_attn_dt_bias", "L0_attn_A_log", "L0_attn_D", "L0_attn_norm_w",
             "L0_attn_out_proj_w", "L0_ffn_router_w", "L0_ffn_A", "L0_ffn_B"]
    in_maps = []
    for c in range(8):
        m = {"hs": np.ascontiguousarray(hs_full[c])}
        for k in wkeys:
            m[k] = np.ascontiguousarray(inputs[k], dtype=np.float32)
        in_maps.append(m)
    res = run_bass_kernel_spmd(nc, in_maps, core_ids=list(range(8)))
    return np.stack([res.results[i]["out"] for i in range(8)], axis=0)


# version 14 (final)
# speedup vs baseline: 1.0335x; 1.0335x over previous
import os
import numpy as np
from collections import defaultdict
from contextlib import ExitStack

_PH = os.environ.get("ATH_PHASE", "full")
_NT_LIM = int(os.environ.get("ATH_NT", "4"))
_ODM = int(os.environ.get("ATH_ODM", "32"))
_OSUB = os.environ.get("ATH_OSUB", "full")

import concourse.bass as bass
import concourse.mybir as mybir
from concourse.bass_utils import run_bass_kernel_spmd

F32 = mybir.dt.float32
BF16 = mybir.dt.bfloat16
AF = mybir.ActivationFunctionType
ALU = mybir.AluOpType
AX = mybir.AxisListType

DM = 4096
DI = 8192
NH = 128
PDim = 64
NG = 8
NS = 128
CK = 4
CH = 256
L = 2048
TT = 512
NT = L // TT
NCH = TT // CH
NK = DM // 128
NF = 145
NFD = DI // 128
CONVD = DI + 2 * NG * NS
NCT = CONVD // 128
XBCF0 = 64
DTF = 144
EPS = 1e-5

ENGINES = ("sync", "gpsimd", "scalar", "vector", "tensor")


class Plan:
    def __init__(self):
        self.steps = {e: [] for e in ENGINES}
        self.cnt = defaultdict(int)

    def add(self, engine, fn, waits=(), inc=None, incn=1):
        tok = None
        if inc is not None:
            self.cnt[inc] += incn
            tok = (inc, self.cnt[inc])
        ws = tuple(w for w in waits if w is not None)
        for (s, v) in ws:
            assert v <= self.cnt[s], f"wait on future token {s}={v}>{self.cnt[s]}"
        self.steps[engine].append((fn, ws, (inc, incn) if inc else None))
        return tok


def build_program():
    nc = bass.Bass("TRN2", target_bir_lowering=False)

    hs = nc.dram_tensor("hs", [L, DM], F32, kind="ExternalInput").ap()
    w_in = nc.dram_tensor("L0_attn_in_proj_w", [NF * 128, DM], F32, kind="ExternalInput").ap()
    conv_w = nc.dram_tensor("L0_attn_conv_w", [CONVD, CK], F32, kind="ExternalInput").ap()
    conv_b = nc.dram_tensor("L0_attn_conv_b", [CONVD], F32, kind="ExternalInput").ap()
    dt_bias = nc.dram_tensor("L0_attn_dt_bias", [NH], F32, kind="ExternalInput").ap()
    a_log = nc.dram_tensor("L0_attn_A_log", [NH], F32, kind="ExternalInput").ap()
    d_in = nc.dram_tensor("L0_attn_D", [NH], F32, kind="ExternalInput").ap()
    norm_w = nc.dram_tensor("L0_attn_norm_w", [DI], F32, kind="ExternalInput").ap()
    w_out = nc.dram_tensor("L0_attn_out_proj_w", [DM, DI], F32, kind="ExternalInput").ap()
    w_r = nc.dram_tensor("L0_ffn_router_w", [4, DM], F32, kind="ExternalInput").ap()
    ffn_a = nc.dram_tensor("L0_ffn_A", [4, 16, DM], F32, kind="ExternalInput").ap()
    ffn_b = nc.dram_tensor("L0_ffn_B", [4, DM, 16], F32, kind="ExternalInput").ap()
    out_h = nc.dram_tensor("out", [L, DM], F32, kind="ExternalOutput").ap()

    win_cache = nc.dram_tensor("win_cache", [NF, 128, DM], BF16, kind="Internal").ap()
    wout_cache = nc.dram_tensor("wout_cache", [NK, 128, DI], BF16, kind="Internal").ap()
    acs_d = nc.dram_tensor("acs_d", [2, 2, NH, CH], BF16, kind="Internal").ap()
    row_d = nc.dram_tensor("row_d", [2, 4096], BF16, kind="Internal").ap()
    ffn_d = nc.dram_tensor("ffn_d", [128, 2048 + 4096 + 128], BF16, kind="Internal").ap()
    cwcb_d = nc.dram_tensor("cwcb_d", [128, 400], F32, kind="Internal").ap()

    P = Plan()
    with ExitStack() as ctx:
        alloc = nc.alloc_sbuf_tensor
        r0 = alloc("r0_xbc", [128, NCT * TT], BF16).ap()
        r1 = alloc("r1_xt", [128, NK * TT], BF16).ap()
        r2 = alloc("r2_big", [128, 16384], BF16).ap()
        r3 = alloc("r3_w", [128, 8192], BF16).ap()
        r4 = alloc("r4_s", [128, NH * PDim], BF16).ap()
        r5 = alloc("r5_a", [128, 4096], BF16).ap()
        r6 = alloc("r6_b", [128, 4096], F32).ap()
        dtproc = alloc("dtproc", [128, TT], F32).ap()
        tailb = alloc("tailb", [128, NCT * 3], BF16).ap()
        zsb = alloc("zsb", [128, 2 * TT], BF16).ap()
        opfm = zsb
        e4 = alloc("e4", [4, 64], BF16).ap()
        identb = alloc("identb", [128, 128], BF16).ap()
        identf = alloc("identf", [128, 128], F32).ap()
        trimask = alloc("trimask", [128, 128], BF16).ap()
        onescol = alloc("onescol", [128, 1], BF16).ap()
        onesf1 = alloc("onesf1", [1, 128], F32).ap()
        nw_col = alloc("nw_col", [128, NFD], F32).ap()
        dpairs = alloc("dpairs", [128, PDim], F32).ap()
        acol = alloc("acol", [128, 1], F32).ap()
        dtbcol = alloc("dtbcol", [128, 1], F32).ap()

        smallf = alloc("smallf", [128, 64], F32).ap()
        logits = alloc("logits", [128, 16], F32).ap()
        gatesb = alloc("gatesb", [128, 16], F32).ap()
        cmp4 = alloc("cmp4", [128, 16], F32).ap()
        cnt4 = alloc("cnt4", [128, 4], F32).ap()
        jlt = alloc("jlt", [128, 16], F32).ap()
        invc = alloc("invc", [128, 4], F32).ap()
        constf = alloc("constf", [128, 8], F32).ap()

        rowbuf = r5[0:1, 0:4096]
        dtt = r6[:, 0:512]
        rmsrow = dtt[0:1, 0:TT]
        onesf256 = r6[:, 3840:4096]
        g2b = r6.bitcast(BF16)[:, 1024:1536]
        g2b2 = [g2b, r6.bitcast(BF16)[:, 1536:2048]]
        rraw = dtproc[0:4, 0:TT]
        gatesT = rraw
        midb = zsb[0:64, 0:TT]
        gatesTb = midb[0:4, 0:TT]
        gmidb = dtproc.bitcast(BF16)[0:64, 0:TT]
        dttT = r5.bitcast(F32)[:, 1024:1280]
        dtdsT = r5.bitcast(F32)[:, 1280:1536]
        cdb = r5.bitcast(F32)[:, 1536:1664]
        dball = cdb
        cdrow = r5.bitcast(F32)[0:1, 1664:1792]
        cw_all = r5.bitcast(F32)[:, 0:320]
        cb_col = r5.bitcast(F32)[:, 320:400]
        sp1x = r5.bitcast(F32)[:, 400:912]
        sp2x = r5.bitcast(F32)[:, 912:1424]
        acat = r3[:, 0:2048]
        bcat = r3[0:64, 2048:6144]
        wrt = r3[:, 6144:6272]
        xt = r1
        xtk = xt.rearrange("p (k t) -> p k t", t=TT)
        lnm = [r1[:, i * 4096:(i + 1) * 4096] for i in range(4)]
        r2f = r2.bitcast(F32)
        wnat32 = [r2f[:, 0:4096], r2f[:, 4096:8192]]
        dhl = [r2[0:4, 0:4096], r2[0:4, 4096:8192]]
        rhs4 = [r2[0:4, 8192:12288], r2[0:4, 12288:16384]]
        woasm = [r2[:, 0:8192], r2[:, 8192:16384]]
        x1fm = r2
        x1fmk = x1fm.rearrange("p (k t) -> p k t", t=TT)
        winT = [r3[:, 0:4096], r3[:, 4096:8192]]
        r3f = r3.bitcast(F32)
        wnatq = [r3f[:, 0:2048], r3f[:, 2048:4096]]
        mxb = [r3[:, i * 256:(i + 1) * 256] for i in range(2)]
        xdtb = [r3[:, 512 + i * 256: 512 + (i + 1) * 256] for i in range(2)]
        dsxdtb = [r3[:, 1024 + i * 256: 1024 + (i + 1) * 256] for i in range(2)]
        r5f = r5.bitcast(F32)
        acsb = r5f[:, 0:256]
        dab = r5f[:, 256:512]
        dsb = r5f[:, 512:768]
        dtdsb = r5f[:, 768:1024]
        ln2b = r5
        r6b = r6.bitcast(BF16)
        gg = [r6b[:, g * 384:(g + 1) * 384] for g in range(NG)]
        btm = [r6b[:, 3072 + g * 256: 3072 + (g + 1) * 256] for g in range(NG)]
        acshi = r6b[:, 5120:5376]
        acslo = r6b[:, 5376:5632]
        lbuf = [r6b[:, 5632 + i * 384: 5632 + (i + 1) * 384] for i in range(2)]
        odbb = [r6b[:, 6400 + i * 256: 6400 + (i + 1) * 256] for i in range(2)]
        mpb = [r6b[:, 6912 + i * 384: 6912 + (i + 1) * 384] for i in range(2)]
        mpb2 = [mpb, [zsb[:, 0:384], zsb[:, 384:768]]]
        mxb2 = [mxb, [zsb[:, 768:1024], r6b[:, 5376:5632]]]
        opfm2 = [zsb[:, 0:512], zsb[:, 512:1024]]
        convtmp = r6[:, 0:512]
        xstage = [r6b[:, 0:4096], r6b[:, 4096:8192]]
        xres = r6
        obuf = r6
        r0j = r0.rearrange("p (j t) -> p j t", t=TT)

        apt = nc.alloc_psum_tensor
        pm = [apt("pm0", [128, 512], F32).ap(), apt("pm1", [128, 512], F32).ap()]
        pt = [apt("pt0", [128, 512], F32).ap(), apt("pt1", [128, 512], F32).ap()]
        pdk = [apt("pd0", [128, 512], F32).ap(), apt("pd1", [128, 512], F32).ap()]
        py = [apt("py0", [128, 512], F32).ap(), apt("py1", [128, 512], F32).ap()]
        ptb = [p.bitcast(BF16) for p in pt]
        pdb = [p.bitcast(BF16) for p in pdk]

        semn = ["te", "dve", "act", "gps",
                "rowd", "cr", "sm", "cwcb", "prep", "ffn",
                "xs0", "xs1", "w0", "w1", "cwin0", "cwin1",
                "img0", "img1", "st0", "st1",
                "wq0", "wq1", "wo0", "wo1", "cwout0", "cwout1",
                "w2", "w3", "wo2", "wo3",
                "xr", "ob"]
        sems = {n: ctx.enter_context(nc.semaphore(n)) for n in semn}
        blk = ctx.enter_context(nc.Block())

        def dma(queue, stream, out, in_, waits=(), **kw):
            def fn(eng, out=out, in_=in_, kw=kw):
                return eng.dma_start(out=out, in_=in_, **kw)
            return P.add(queue, fn, waits=waits, inc=stream, incn=16)

        def op(engine, meth, *args, waits=(), inc=True, **kw):
            sem = {"tensor": "te", "vector": "dve", "scalar": "act",
                   "gpsimd": "gps"}[engine]
            def fn(eng, meth=meth, args=args, kw=kw):
                return getattr(eng, meth)(*args, **kw)
            selfw = ((sem, P.cnt[sem]),) if P.cnt[sem] else ()
            return P.add(engine, fn, waits=tuple(waits) + selfw, inc=sem)

        def mm(out, lhsT, rhsX, waits=(), inc=True, start=True, stop=True, tp=None,
               skip=False):
            kw = dict(start=start, stop=stop)
            if tp is not None:
                kw["tile_position"] = tp
            if skip:
                kw["skip_group_check"] = True
            def fn(eng, out=out, lhsT=lhsT, rhsX=rhsX, kw=kw):
                return eng.matmul(out, lhsT, rhsX, **kw)
            return P.add("tensor", fn, waits=waits, inc="te" if inc else None)

        def tr(out, in_, waits=(), inc=True):
            p = in_.shape[0]
            idt = identf if in_.dtype == F32 else identb
            def fn(eng, out=out, in_=in_, p=p, idt=idt):
                return eng.transpose(out, in_, idt[0:p, 0:p])
            return P.add("tensor", fn, waits=waits, inc="te" if inc else None)

        tk = op("gpsimd", "memset", identb, 1.0)
        tk = op("gpsimd", "affine_select", out=identb, in_=identb,
                compare_op=ALU.is_ge, fill=0.0, base=0, pattern=[[-1, 128]],
                channel_multiplier=1, waits=[tk])
        tk = op("gpsimd", "affine_select", out=identb, in_=identb,
                compare_op=ALU.is_ge, fill=0.0, base=0, pattern=[[1, 128]],
                channel_multiplier=-1, waits=[tk])
        tk = op("gpsimd", "tensor_copy", identf, identb, waits=[tk])
        tk = op("gpsimd", "memset", trimask, 0.0, waits=[tk])
        tk = op("gpsimd", "affine_select", out=trimask, in_=trimask,
                compare_op=ALU.is_ge, fill=-1e30, base=0, pattern=[[1, 128]],
                channel_multiplier=-1, waits=[tk])
        tk = op("gpsimd", "memset", e4, 1.0, waits=[tk])
        tk = op("gpsimd", "affine_select", out=e4, in_=e4, compare_op=ALU.is_ge,
                fill=0.0, base=0, pattern=[[1, 64]], channel_multiplier=-16,
                waits=[tk])
        tk = op("gpsimd", "affine_select", out=e4, in_=e4, compare_op=ALU.is_ge,
                fill=0.0, base=15, pattern=[[-1, 64]], channel_multiplier=16,
                waits=[tk])
        op("gpsimd", "memset", onescol, 1.0, inc=False)
        op("gpsimd", "memset", onesf1, 1.0, inc=False)
        op("gpsimd", "memset", r4, 0.0, inc=False)
        op("gpsimd", "memset", tailb, 0.0, inc=False)
        for e in range(4):
            for j in range(4):
                op("gpsimd", "memset", jlt[:, e * 4 + j: e * 4 + j + 1],
                   1.0 if j < e else 0.0, inc=False)
        cvals = [1.0, -1.0, -0.5, 4.0, EPS, 1.0 / DI, 2.0, 0.5]
        for ci, cv in enumerate(cvals):
            op("gpsimd", "memset", constf[:, ci:ci + 1], cv, inc=False)
            nc.const_aps.register(F32, cv, constf[:, ci:ci + 1])
        tok_setup = op("gpsimd", "memset", smallf, 0.0, waits=[tk])

        op("vector", "memset", rowbuf[0:1, 0:2048], 1.0, inc=False)
        t_row1 = op("vector", "memset", rowbuf[0:1, 2048:4096], -1.0)
        t_rowd = dma("sync", "rowd", row_d[0:1, 0:2048], rowbuf[0:1, 0:2048],
                     waits=[t_row1])
        t_rowd = dma("sync", "rowd", row_d[0:1, 2048:4096], rowbuf[0:1, 0:2048],
                     waits=[t_rowd])
        t_rowd = dma("sync", "rowd", row_d[1:2, 0:2048], rowbuf[0:1, 2048:4096],
                     waits=[t_rowd])
        t_rowd = dma("sync", "rowd", row_d[1:2, 2048:4096], rowbuf[0:1, 2048:4096],
                     waits=[t_rowd])

        t_dtb = dma("scalar", "sm", dtbcol, dt_bias.unsqueeze(1))
        t_al = dma("scalar", "sm", acol, a_log.unsqueeze(1), waits=[t_dtb])
        t_drow = dma("scalar", "sm", rowbuf.bitcast(F32)[0:1, 0:128],
                     d_in.unsqueeze(0), waits=[t_rowd, t_al])
        t_aexp = op("scalar", "activation", out=acol, in_=acol, func=AF.Exp,
                    waits=[t_al])
        t_aneg = op("vector", "tensor_scalar_mul", acol, acol, -1.0, waits=[t_aexp])
        t_dmm = mm(pm[0][:, 0:128], onesf1, rowbuf.bitcast(F32)[0:1, 0:128],
                   waits=[t_drow, tok_setup])
        t_dball = op("vector", "tensor_copy", dball, pm[0][:, 0:128], waits=[t_dmm])
        t_dp1 = op("vector", "tensor_copy", dpairs[0:64, :], dball[0:64, 0:128:2],
                   waits=[t_dball])
        t_dp2 = op("vector", "tensor_copy", dpairs[64:128, :], dball[64:128, 1:128:2],
                   waits=[t_dp1])
        t_cw = t_dmm
        for part in range(4):
            t_cw = dma("sync", "sm",
                       cw_all[32 * part:32 * part + 32, :].rearrange(
                           "p (i k) -> p i k", k=CK),
                       conv_w.rearrange("(i q p) k -> q p i k", q=4, p=32)[part],
                       waits=[t_cw], allow_slow_non_contiguous=True)
        t_cbn = dma("sync", "sm", r6[0:80, 0:128],
                    conv_b.rearrange("(i p) -> i p", p=128), waits=[t_dp2, t_cw])
        t_cbt = tr(pm[1][:, 0:80], r6[0:80, 0:128], waits=[t_cbn, tok_setup])
        t_cb = op("vector", "tensor_copy", cb_col, pm[1][:, 0:80], waits=[t_cbt])
        t_cwcb = dma("sync", "cwcb", cwcb_d, r5f[:, 0:400], waits=[t_cw, t_cb])
        t_nwn = dma("sync", "sm", r6[0:64, 128:256],
                    norm_w.rearrange("(i p) -> i p", p=128), waits=[t_cb, t_cw])
        t_nwt = tr(pm[1][:, 128:192], r6[0:64, 128:256], waits=[t_nwn, t_cb])
        t_nw = op("vector", "tensor_copy", nw_col, pm[1][:, 128:192], waits=[t_nwt])

        t_an = dma("gpsimd", "prep", r6b[0:64, 0:4096],
                   ffn_a.rearrange("e r d -> (e r) d"), waits=[t_nw])
        prev_ev = t_nw
        for k in range(NK):
            tk = tr(ptb[0][:, 0:64], r6b[0:64, k * 128:(k + 1) * 128],
                    waits=[t_an, prev_ev])
            prev_ev = op("vector", "tensor_copy", acat[:, k * 64:(k + 1) * 64],
                         ptb[0][:, 0:64], waits=[tk])
        t_wrn = dma("gpsimd", "prep", r6b[0:4, 4096:8192], w_r,
                     waits=[prev_ev, t_an])
        for k in range(NK):
            tk = tr(ptb[0][:, 0:4], r6b[0:4, 4096 + k * 128: 4096 + (k + 1) * 128],
                    waits=[t_wrn, prev_ev])
            prev_ev = op("vector", "tensor_copy", wrt[:, k * 4:(k + 1) * 4],
                         ptb[0][:, 0:4], waits=[tk])
        t_bn_prev = prev_ev
        for oc in range(NK):
            src = ffn_b.rearrange("e (oc p) r -> oc p e r", p=128)[oc]
            tn = dma("gpsimd", "prep",
                     r6b[0:128, 0:64].rearrange("p (e r) -> p e r", r=16),
                     src, waits=[t_bn_prev, t_wrn])
            tk = tr(ptb[0][0:64, 0:128], r6b[0:128, 0:64],
                    waits=[tn, prev_ev])
            prev_ev = op("scalar", "mul", bcat[:, oc * 128:(oc + 1) * 128],
                         ptb[0][0:64, 0:128], 4.0, waits=[tk])
            t_bn_prev = prev_ev
        t_ffns = dma("sync", "ffn", ffn_d[:, 0:2048], acat, waits=[prev_ev])
        t_ffns = dma("sync", "ffn", ffn_d[0:64, 2048:6144], bcat, waits=[t_ffns])
        t_ffns = dma("sync", "ffn", ffn_d[:, 6144:6272], wrt, waits=[t_ffns])
        t_prep_done = op("vector", "engine_nop", waits=[prev_ev, t_ffns])

        tok_r0_free = None
        tok_r1_free = None
        tok_r2_free = None
        tok_r3_free = t_prep_done
        tok_r5_free = None
        tok_r6_free = None
        tok_pm = [t_dball, t_nw]
        t_refill_last = [None, None]
        t_batch_mm = [None] * 8
        t_supd_prev = [None] * 64
        t_out_last = None
        t_prev_obuf = None
        t_cache_win = [None, None]
        t_cache_wout = [None, None]

        for tt in range(min(NT, _NT_LIM)):
            pass1 = (tt == 0)
            c0 = tt * TT
            last_tile = (tt == min(NT, _NT_LIM) - 1)

            t_xtev = []
            t_xload = []
            for ts in range(4):
                w = []
                if tt == 0 and ts == 0:
                    w.append(t_prep_done)
                if ts == 0:
                    if tok_r1_free is not None:
                        w.append(tok_r1_free)
                    if tok_r6_free is not None:
                        w.append(tok_r6_free)
                if len(t_xload) >= 2:
                    w.append(t_xtev[-1])
                tn = dma("gpsimd", "xs%d" % (ts % 2), xstage[ts % 2],
                         hs[c0 + ts * 128: c0 + (ts + 1) * 128, :], waits=w)
                t_xload.append(tn)
                for kg in range(8):
                    tts = []
                    for k4 in range(4):
                        k = kg * 4 + k4
                        w2 = [tn]
                        if len(t_xtev) >= 2 and k4 == 0:
                            w2.append(t_xtev[-2])
                        tts.append(tr(ptb[kg % 2][:, k4 * 128:(k4 + 1) * 128],
                                      xstage[ts % 2][:, k * 128:(k + 1) * 128],
                                      waits=w2))
                    t_xtev.append(op(
                        "vector", "tensor_copy",
                        xtk[:, kg * 4:(kg + 1) * 4, ts * 128:(ts + 1) * 128],
                        ptb[kg % 2].rearrange("p (k c) -> p k c", c=128)[:, 0:4, :],
                        waits=tts))
            t_xt_done = t_xtev[-1]

            def acquire_win(f, buf, wprev, extra=()):
                if pass1:
                    w = list(extra) + ([wprev] if wprev else [])
                    ta = dma("sync", "w%d" % buf, wnat32[buf][:, 0:2048],
                             w_in[f * 128:(f + 1) * 128, 0:2048], waits=w)
                    tb = dma("sync", "w%d" % (buf + 2),
                             wnat32[buf][:, 2048:4096],
                             w_in[f * 128:(f + 1) * 128, 2048:4096], waits=w)
                    evs = []
                    for kg in range(8):
                        tts = []
                        for k4 in range(4):
                            k = kg * 4 + k4
                            w2 = [ta if k < 16 else tb]
                            if len(evs) >= 2 and k4 == 0:
                                w2.append(evs[-2])
                            tts.append(tr(pdk[kg % 2][:, k4 * 128:(k4 + 1) * 128],
                                          wnat32[buf][:, k * 128:(k + 1) * 128],
                                          waits=w2))
                        evw = list(tts)
                        if kg == 0 and t_cache_win[buf] is not None:
                            evw.append(t_cache_win[buf])
                        evs.append(op(
                            "vector", "tensor_copy",
                            winT[buf].rearrange("p (k c) -> p k c", c=128)[:, kg * 4:(kg + 1) * 4, :],
                            pdk[kg % 2].rearrange("p (k c) -> p k c", c=128)[:, 0:4, :],
                            waits=evw))
                    t_cache_win[buf] = dma("sync", "cwin%d" % buf, win_cache[f],
                                           winT[buf], waits=[evs[-1]])
                    return evs[-1]
                else:
                    w = list(extra) + ([wprev] if wprev else [])
                    ta = dma("sync", "w%d" % buf, winT[buf][:, 0:2048],
                             win_cache[f][:, 0:2048], waits=w)
                    tb = dma("sync", "w%d" % (buf + 2),
                             winT[buf][:, 2048:4096],
                             win_cache[f][:, 2048:4096], waits=w)
                    return [ta, tb]

            def inproj_ftile(buf, wk, pm_extra):
                wkl = wk if isinstance(wk, list) else [wk]
                mm_w = wkl + [t_xt_done] + pm_extra
                t = None
                for k in range(NK):
                    t = mm(pm[buf][:, 0:TT], winT[buf][:, k * 128:(k + 1) * 128],
                           xtk[:, k, :], waits=mm_w if k == 0 else (),
                           inc=(k == NK - 1), start=(k == 0), stop=(k == NK - 1))
                return t

            t_win_used = [None, None]
            t_pm = [tok_pm[0], tok_pm[1]]
            conv_last = None
            t_dt_evac = None
            w_cwl = [t_cwcb]
            if tok_r5_free is not None:
                w_cwl.append(tok_r5_free)
            t_cwl = dma("sync", "cwcb", r5f[:, 0:400], cwcb_d, waits=w_cwl)
            for idx, f in enumerate(range(XBCF0, DTF + 1)):
                buf = idx % 2
                extra = []
                if idx <= 1:
                    if tok_r3_free is not None:
                        extra.append(tok_r3_free)
                    if pass1 and tok_r2_free is not None:
                        extra.append(tok_r2_free)
                    if pass1 and tt == 0:
                        extra.append(t_rowd)
                wk = acquire_win(f, buf, t_win_used[buf], extra=extra)
                t_g32 = inproj_ftile(buf, wk, [t_pm[buf]])
                t_win_used[buf] = t_g32
                if f == DTF:
                    t_dt_evac = op("vector", "tensor_copy", dtt, pm[buf][:, 0:TT],
                                   waits=[t_g32, conv_last])
                    t_pm[buf] = t_dt_evac
                else:
                    j = f - XBCF0
                    xtile = r0j[:, j, :]
                    w_ev = [t_g32]
                    if idx == 0 and tok_r0_free is not None:
                        w_ev.append(tok_r0_free)
                    t_ev = op("vector", "tensor_copy", xtile, pm[buf][:, 0:TT],
                              waits=w_ev)
                    tl = tailb[:, j * 3: j * 3 + 3]
                    w0 = cw_all[:, j * 4:j * 4 + 1]
                    w1 = cw_all[:, j * 4 + 1:j * 4 + 2]
                    w2c = cw_all[:, j * 4 + 2:j * 4 + 3]
                    w3 = cw_all[:, j * 4 + 3:j * 4 + 4]
                    t_c0 = op("scalar", "activation", out=convtmp, in_=xtile,
                              func=AF.Identity, bias=cb_col[:, j:j + 1], scale=w3,
                              waits=[t_ev, t_cwl])
                    t_c1 = op("vector", "scalar_tensor_tensor", convtmp[:, 1:TT],
                              xtile[:, 0:TT - 1], w2c, convtmp[:, 1:TT],
                              ALU.mult, ALU.add, waits=[t_c0])
                    t_c2 = op("vector", "scalar_tensor_tensor", convtmp[:, 2:TT],
                              xtile[:, 0:TT - 2], w1, convtmp[:, 2:TT],
                              ALU.mult, ALU.add, waits=[t_c1])
                    t_c3 = op("vector", "scalar_tensor_tensor", convtmp[:, 3:TT],
                              xtile[:, 0:TT - 3], w0, convtmp[:, 3:TT],
                              ALU.mult, ALU.add, waits=[t_c2])
                    t_b0 = op("vector", "scalar_tensor_tensor", convtmp[:, 0:3],
                              tl, w0, convtmp[:, 0:3], ALU.mult, ALU.add,
                              waits=[t_c3])
                    t_b1 = op("vector", "scalar_tensor_tensor", convtmp[:, 0:2],
                              tl[:, 1:3], w1, convtmp[:, 0:2], ALU.mult, ALU.add,
                              waits=[t_b0])
                    t_b2 = op("vector", "scalar_tensor_tensor", convtmp[:, 0:1],
                              tl[:, 2:3], w2c, convtmp[:, 0:1], ALU.mult, ALU.add,
                              waits=[t_b1])
                    t_tl2 = op("vector", "tensor_copy", tl, xtile[:, TT - 3:TT],
                               waits=[t_b2])
                    t_si = op("scalar", "activation", out=xtile, in_=convtmp,
                              func=AF.Silu, waits=[t_tl2])
                    t_pm[buf] = t_ev
                    conv_last = t_si
            t_wk_last = t_win_used[0]

            t_s0 = op("vector", "tensor_scalar_add", sp1x, dtt, dtbcol,
                      waits=[t_dt_evac, t_dtb])
            t_s1 = op("scalar", "activation", out=sp2x, in_=sp1x, func=AF.Abs,
                      waits=[t_s0])
            t_s2 = op("scalar", "activation", out=sp2x, in_=sp2x, func=AF.Exp,
                      scale=-1.0, waits=[t_s1])
            t_s3 = op("scalar", "activation", out=sp2x, in_=sp2x, func=AF.Ln, bias=1.0,
                      waits=[t_s2])
            t_s4 = op("vector", "tensor_single_scalar", sp1x, sp1x, 0.0, ALU.max,
                      waits=[t_s3])
            t_dtp = op("vector", "tensor_add", dtproc, sp1x, sp2x, waits=[t_s4])
            if _PH == "a" and last_tile:
                t_out_last = dma("scalar", "ob", out_h[0:128, :], obuf,
                                 waits=[t_dtp, conv_last])
                break

            w_cr = [t_rowd]
            if pass1:
                w_cr.append(t_win_used[0])
                w_cr.append(t_win_used[1])
            elif tok_r2_free is not None:
                w_cr.append(tok_r2_free)
            t_const = []
            for i in range(2):
                t_const.append(dma("sync", "cr", dhl[i][0:2, :],
                                   row_d[0:1, :].broadcast_to([2, 4096]),
                                   waits=w_cr + (t_const[-1:] or [])))
                t_const.append(dma("sync", "cr", rhs4[i][2:4, :],
                                   row_d[1:2, :].broadcast_to([2, 4096]),
                                   waits=[t_const[-1]]))

            t_y_evac = {}
            t_zwaits = []
            t_mp = [None, None]
            t_yev = [None, None]
            t_ones256 = op("gpsimd", "memset", onesf256, 1.0, waits=[t_xt_done])
            for cc in range(NCH):
                par = (tt * NCH + cc) % 2
                ccol = cc * CH
                t_da = op("vector", "tensor_scalar_mul", dab,
                          dtproc[:, ccol:ccol + CH], acol, waits=[t_dtp, t_aneg])
                t_acs = op("vector", "tensor_tensor_scan", acsb, dab, onesf256, 0.0,
                           ALU.add, ALU.mult, waits=[t_da, t_ones256])
                t_hi = op("vector", "tensor_copy", acshi, acsb, waits=[t_acs])
                t_lo32 = op("vector", "tensor_tensor", dsb, acsb, acshi,
                            ALU.subtract, waits=[t_hi])
                t_lo = op("vector", "tensor_copy", acslo, dsb, waits=[t_lo32])
                w_img = [t_refill_last[par]] if t_refill_last[par] else []
                t_wh = dma("scalar", "img%d" % par, acs_d[par, 0], acshi,
                           waits=[t_hi] + w_img)
                t_wl = dma("scalar", "img%d" % par, acs_d[par, 1], acslo,
                           waits=[t_lo, t_wh] + w_img)
                t_ds = op("scalar", "activation", out=dsb, in_=acsb, func=AF.Exp,
                          scale=-1.0, bias=acsb[:, CH - 1:CH], waits=[t_lo32])
                t_dtds = op("vector", "tensor_tensor", dtdsb, dsb,
                            dtproc[:, ccol:ccol + CH], ALU.mult, waits=[t_ds])
                t_cd0 = op("scalar", "activation", out=smallf[:, 32:33],
                           in_=acsb[:, CH - 1:CH], func=AF.Exp, waits=[t_acs])
                t_cdt = tr(pt[0][0:1, 0:128], smallf[:, 32:33],
                           waits=[t_cd0] + t_zwaits)
                t_cdr = op("vector", "tensor_copy", cdrow, pt[0][0:1, 0:128],
                           waits=[t_cdt])
                t_cdbm = mm(pm[0][:, 0:128], onesf1, cdrow,
                            waits=[t_cdr, t_pm[0]])
                t_cdb = op("vector", "tensor_copy", cdb, pm[0][:, 0:128],
                           waits=[t_cdbm])
                t_pm[0] = t_cdb
                tk1 = tr(pt[1][:, 0:128], dtproc[:, ccol:ccol + 128],
                         waits=[t_dtp] + t_zwaits)
                tk2 = tr(pt[1][:, 128:256], dtproc[:, ccol + 128:ccol + 256])
                t_dttT = op("vector", "tensor_copy", dttT, pt[1][:, 0:256],
                            waits=[tk2])
                tk3 = tr(pt[1][:, 256:384], dtdsb[:, 0:128],
                         waits=[t_dtds, t_dttT])
                tk4 = tr(pt[1][:, 384:512], dtdsb[:, 128:256])
                t_ddtT = op("vector", "tensor_copy", dtdsT, pt[1][:, 256:512],
                            waits=[tk4])
                t_gg = []
                t_btm_ev = None
                for g in range(NG):
                    bt = r0j[:, 64 + g, ccol:ccol + CH]
                    ct = r0j[:, 72 + g, ccol:ccol + CH]
                    w = []
                    if g == 0:
                        w = [conv_last, t_pm[1]]
                    if t_gg:
                        w = [t_gg[-1]]
                    mm(pm[1][:, 0:256], bt[:, 0:128], ct, waits=w, inc=False)
                    tkg = mm(pm[1][:, 256:384], bt[:, 128:256], ct[:, 128:256],
                             skip=True)
                    w_ge = [tkg]
                    if tt == 0 and cc == 0 and g == 0:
                        w_ge.append(t_aneg)
                    if tt == 0 and cc == 0 and g < 3:
                        w_ge.append(t_prep_done)
                    t_gg.append(op("vector", "tensor_copy", gg[g], pm[1][:, 0:384],
                                   waits=w_ge))
                    wbt = [t_ddtT] if g == 0 else [t_btm_ev]
                    b1 = tr(ptb[1][:, 0:128], bt[:, 0:128], waits=wbt)
                    b2 = tr(ptb[1][:, 128:256], bt[:, 128:256])
                    t_btm_ev = op("vector", "tensor_copy", btm[g],
                                  ptb[1][:, 0:256], waits=[b2])
                t_pm[1] = t_gg[-1]
                t_stage = [None] * 8

                def refill(b):
                    sb = b % 2
                    src = acs_d[par].rearrange("a h c -> a (h c)")[:, b * 4096:(b + 1) * 4096]
                    w = [t_wh, t_wl] + list(t_const)
                    if t_batch_mm[b - 2] is not None:
                        w.append(t_batch_mm[b - 2])
                    t1 = dma("sync", "st%d" % sb, dhl[sb][2:4, :], src, waits=w)
                    t2 = dma("sync", "st%d" % sb, rhs4[sb][0:2, :], src,
                             waits=[t1])
                    t_stage[b] = t2
                    return t2

                t_refill_last[par] = refill(0)

                t_actL = [None, None]
                t_actO = [None, None]
                t_mp = [None, None]
                t_mx = [None, None]
                t_dsx = [None, None]
                t_ymm = [None, None]
                t_yev = [None, None]
                t_sup = [None, None]
                t_supd_cur = [None] * 64
                pend = {}
                for it in range(65):
                  if it < 64:
                    pr = it
                    pp = pr % 2
                    hA, hB = 2 * pr, 2 * pr + 1
                    g = hA // 16
                    if pr % 8 == 0 and pr // 8 + 1 < 8:
                        t_refill_last[par] = refill(pr // 8 + 1)
                    xcp = r0j[:, pr, ccol:ccol + CH]
                    w = []
                    if t_dsx[pp] is not None:
                        w.append(t_dsx[pp])
                    if pr == 0:
                        w += [t_btm_ev, t_dttT, t_ddtT]
                    xt1 = tr(ptb[pp][:, 0:128], xcp[:, 0:128], waits=w)
                    xt2 = tr(ptb[pp][:, 128:256], xcp[:, 128:256])
                    wx = [xt2, t_dttT]
                    if t_ymm[pp] is not None:
                        wx.append(t_ymm[pp])
                    if pr <= 1 and tok_r3_free is not None and tt + cc == 0:
                        wx.append(tok_r3_free)
                    if pr == 0 and cc == 0 and pass1:
                        wx += [t for t in t_cache_win if t is not None]
                    t_x1 = op("vector", "tensor_tensor",
                              xdtb[pp][:, 0:128].rearrange("p (h x) -> p h x", h=2),
                              ptb[pp][:, 0:128].rearrange("p (h x) -> p h x", h=2),
                              dttT[:, 2 * pr:2 * pr + 2].unsqueeze(2).broadcast_to([128, 2, 64]),
                              ALU.mult, waits=wx)
                    t_x2 = op("vector", "tensor_tensor",
                              xdtb[pp][:, 128:256].rearrange("p (h x) -> p h x", h=2),
                              ptb[pp][:, 128:256].rearrange("p (h x) -> p h x", h=2),
                              dttT[:, 128 + 2 * pr:128 + 2 * pr + 2].unsqueeze(2).broadcast_to([128, 2, 64]),
                              ALU.mult, waits=[t_x1])
                    t_d1 = op("vector", "tensor_tensor",
                              dsxdtb[pp][:, 0:128].rearrange("p (h x) -> p h x", h=2),
                              ptb[pp][:, 0:128].rearrange("p (h x) -> p h x", h=2),
                              dtdsT[:, 2 * pr:2 * pr + 2].unsqueeze(2).broadcast_to([128, 2, 64]),
                              ALU.mult, waits=[t_x2, t_ddtT])
                    t_d2 = op("vector", "tensor_tensor",
                              dsxdtb[pp][:, 128:256].rearrange("p (h x) -> p h x", h=2),
                              ptb[pp][:, 128:256].rearrange("p (h x) -> p h x", h=2),
                              dtdsT[:, 128 + 2 * pr:128 + 2 * pr + 2].unsqueeze(2).broadcast_to([128, 2, 64]),
                              ALU.mult, waits=[t_d1])
                    t_dsx[pp] = t_d2

                    t_mp_pair = []
                    t_mx_pair = []
                    for h in (hA, hB):
                        hp = h % 2
                        batch = h // 16
                        sb = batch % 2
                        bi = h % 16
                        w = [t_stage[batch]]
                        if t_actL[hp] is not None:
                            w.append(t_actL[hp])
                        mm(pdk[hp][:, 0:256], dhl[sb][:, bi * 256:bi * 256 + 128],
                           rhs4[sb][:, bi * 256:bi * 256 + 256],
                           waits=w, inc=False, start=True, stop=False)
                        mm(pdk[hp][:, 0:128], identb, trimask, waits=[tok_setup],
                           inc=False, start=False, stop=True)
                        mm(pdk[hp][:, 256:384],
                           dhl[sb][:, bi * 256 + 128:bi * 256 + 256],
                           rhs4[sb][:, bi * 256 + 128:bi * 256 + 256],
                           inc=False, start=True, stop=False, skip=True)
                        t_dmm = mm(pdk[hp][:, 256:384], identb, trimask,
                                   start=False, stop=True, skip=True)
                        w2 = []
                        if pr == 0:
                            w2.append(t_pm[hp])
                        if t_actO[hp] is not None:
                            w2.append(t_actO[hp])
                        t_omm = mm(pm[hp][:, 0:256],
                                   dhl[sb][0:2, 0:128],
                                   rhs4[sb][0:2, bi * 256:bi * 256 + 256], waits=w2)
                        t_batch_mm[batch] = t_omm
                        wL = [t_dmm]
                        if t_mp[hp] is not None:
                            wL.append(t_mp[hp])
                        t_L = op("scalar", "activation", out=lbuf[hp],
                                 in_=pdk[hp][:, 0:384], func=AF.Exp, waits=wL)
                        t_actL[hp] = t_L
                        wO = [t_omm]
                        if t_mx[hp] is not None:
                            wO.append(t_mx[hp])
                        t_O = op("scalar", "activation", out=odbb[hp],
                                 in_=pm[hp][:, 0:256], func=AF.Exp, waits=wO)
                        t_actO[hp] = t_O
                        wM = [t_L, t_gg[g]]
                        if t_ymm[pp] is not None:
                            wM.append(t_ymm[pp])
                        t_M = op("gpsimd", "tensor_tensor", mpb2[pp][hp],
                                 lbuf[hp], gg[g], ALU.mult, waits=wM)
                        t_mp[hp] = t_M
                        ct = r0j[:, 72 + g, ccol:ccol + CH]
                        wXx = [t_O]
                        if pr == 0 and cc == 0 and pass1:
                            wXx += [t for t in t_cache_win if t is not None]
                        t_X = op("gpsimd", "tensor_tensor", mxb2[pp][hp],
                                 odbb[hp], ct, ALU.mult, waits=wXx)
                        t_mx[hp] = t_X
                        t_mp_pair.append(t_M)
                        t_mx_pair.append(t_X)
                    pend[pr] = (t_d2, t_mp_pair, t_mx_pair)

                  if it >= 1:
                    pr = it - 1
                    pp = pr % 2
                    hA, hB = 2 * pr, 2 * pr + 1
                    g = hA // 16
                    xcp = r0j[:, pr, ccol:ccol + CH]
                    t_d2p, t_mpp, t_mxp = pend.pop(pr)
                    wY = [t_d2p] + t_mpp + t_mxp
                    if t_yev[pp] is not None:
                        wY.append(t_yev[pp])
                    if t_sup[pp] is not None:
                        wY.append(t_sup[pp])
                    if t_supd_prev[pr] is not None:
                        wY.append(t_supd_prev[pr])
                    mm(py[pp][0:64, 0:256], xdtb[pp][:, 0:64],
                       mpb2[pp][0][:, 0:256],
                       waits=wY, inc=False, start=True, stop=False, tp=(0, 0),
                       skip=True)
                    mm(py[pp][0:64, 128:256], xdtb[pp][:, 128:192],
                       mpb2[pp][0][:, 256:384], inc=False, start=False,
                       stop=False, tp=(0, 0), skip=True)
                    mm(py[pp][0:64, 0:256], r4[:, hA * 64:(hA + 1) * 64],
                       mxb2[pp][0],
                       inc=False, start=False, stop=True, tp=(0, 0), skip=True)
                    mm(py[pp][64:128, 0:256], xdtb[pp][:, 64:128],
                       mpb2[pp][1][:, 0:256],
                       inc=False, start=True, stop=False, tp=(0, 64), skip=True)
                    mm(py[pp][64:128, 128:256], xdtb[pp][:, 192:256],
                       mpb2[pp][1][:, 256:384], inc=False, start=False,
                       stop=False, tp=(0, 64), skip=True)
                    mm(py[pp][64:128, 0:256], r4[:, hB * 64:(hB + 1) * 64],
                       mxb2[pp][1],
                       inc=False, start=False, stop=True, tp=(0, 64), skip=True)
                    mm(py[pp][:, 256:384], btm[g][:, 0:128], dsxdtb[pp][:, 0:128],
                       inc=False, start=True, stop=False, skip=True)
                    t_ymm[pp] = mm(py[pp][:, 256:384], btm[g][:, 128:256],
                                   dsxdtb[pp][:, 128:256], start=False, stop=True,
                                   skip=True)
                    t_ye = op("vector", "scalar_tensor_tensor", xcp, xcp,
                              dpairs[:, pr:pr + 1], py[pp][:, 0:256],
                              ALU.mult, ALU.add, waits=[t_ymm[pp], t_dp2])
                    t_yev[pp] = t_ye
                    t_y_evac[(cc, pr)] = t_ye
                    t_sa = op("vector", "scalar_tensor_tensor",
                              r4[:, hA * 64:(hA + 1) * 64],
                              r4[:, hA * 64:(hA + 1) * 64],
                              cdb[:, hA:hA + 1], py[pp][:, 256:320],
                              ALU.mult, ALU.add, waits=[t_ymm[pp], t_cdb])
                    t_sb = op("vector", "scalar_tensor_tensor",
                              r4[:, hB * 64:(hB + 1) * 64],
                              r4[:, hB * 64:(hB + 1) * 64],
                              cdb[:, hB:hB + 1], py[pp][:, 320:384],
                              ALU.mult, ALU.add, waits=[t_sa])
                    t_sup[pp] = t_sb
                    t_supd_cur[pr] = t_sb
                t_supd_prev = t_supd_cur
                t_pm = [t_actO[0], t_actO[1]]
                t_zwaits = [t_ymm[0], t_ymm[1]]
            if _PH == "b" and last_tile:
                t_out_last = dma("scalar", "ob", out_h[0:128, :], obuf,
                                 waits=t_zwaits + [t_yev[0], t_yev[1], t_sup[0], t_sup[1]])
                break

            t_win_used = [None, None]
            t_rms_mm = None
            t_rmsp = [None, None]
            t_gate_last = None
            t_zs = [None, None]
            pend_z = {}
            for itz in range(NFD + 1):
              if itz < NFD:
                f = itz
                buf = f % 2
                extra = list(t_zwaits) if f <= 1 else []
                wk = acquire_win(f, buf, t_win_used[buf], extra=extra)
                pm_extra = [t_pm[buf]]
                if f == 0:
                    pm_extra += [t for t in t_yev if t is not None]
                t_g32 = inproj_ftile(buf, wk, pm_extra)
                t_win_used[buf] = t_g32
                zs = opfm2[f % 2][:, 0:TT]
                w_zs = [t_g32]
                if t_zs[f % 2] is not None:
                    w_zs.append(t_zs[f % 2])
                t_z = op("scalar", "activation", out=zs, in_=pm[buf][:, 0:TT],
                         func=AF.Silu, waits=w_zs)
                t_pm[buf] = t_z
                pend_z[f] = t_z
              if itz >= 1:
                f = itz - 1
                t_z = pend_z.pop(f)
                zs = opfm2[f % 2][:, 0:TT]
                ytile = r0j[:, f, :]
                wg = [t_z]
                for cc2 in range(NCH):
                    if (cc2, f) in t_y_evac:
                        wg.append(t_y_evac[(cc2, f)])
                t_g = op("vector", "tensor_tensor", ytile, ytile, zs, ALU.mult,
                         waits=wg)
                t_zs[f % 2] = t_g
                w_g2 = [t_g]
                if t_rmsp[f % 2] is not None:
                    w_g2.append(t_rmsp[f % 2])
                t_g2 = op("vector", "tensor_tensor", g2b2[f % 2], ytile, ytile,
                          ALU.mult, waits=w_g2)
                w_rmm = [t_g2]
                if f == 0:
                    w_rmm.append(t_cdr)
                t_rms_mm = mm(pt[0][0:1, 0:TT], onescol, g2b2[f % 2],
                              waits=w_rmm, start=(f == 0), stop=(f == NFD - 1))
                t_rmsp[f % 2] = t_rms_mm
                t_gate_last = t_g
            t_zmm_last = t_win_used[1]
            t_rrow = op("vector", "tensor_copy", rmsrow, pt[0][0:1, 0:TT],
                        waits=[t_rms_mm])
            t_ln1 = op("scalar", "activation", out=rmsrow, in_=rmsrow, func=AF.Ln,
                       scale=1.0 / DI, bias=EPS, waits=[t_rrow])
            t_ex1 = op("scalar", "activation", out=rmsrow, in_=rmsrow, func=AF.Exp,
                       scale=-0.5, waits=[t_ln1])
            t_it = None
            for ts in range(4):
                t_it = tr(pt[0][:, 256 + ts:257 + ts],
                          rmsrow[:, ts * 128:(ts + 1) * 128],
                          waits=[t_ex1] if ts == 0 else ())
            t_inv = op("vector", "tensor_copy", invc, pt[0][:, 256:260],
                       waits=[t_it])
            if _PH == "z" and last_tile:
                t_out_last = dma("scalar", "ob", out_h[0:128, :], obuf,
                                 waits=[t_inv, t_gate_last])
                break

            def acquire_wout(dmt, buf, wprev, extra=()):
                if pass1:
                    evg = []
                    grp = 0
                    last = None
                    lastv = None
                    lasts = None
                    for q in range(4):
                        w = list(extra) + ([wprev] if wprev and q == 0 else [])
                        if wout_qprev[q % 2] is not None:
                            w.append(wout_qprev[q % 2])
                        tn = dma("sync", "wq%d" % (q % 2), wnatq[q % 2],
                                 w_out[dmt * 128:(dmt + 1) * 128,
                                       q * 2048:(q + 1) * 2048], waits=w)
                        if _OSUB == "dma":
                            wout_qprev[q % 2] = tn
                            last = tn
                            continue
                        for fg in range(4):
                            tts = []
                            for f4 in range(4):
                                fs = fg * 4 + f4
                                w2 = [tn]
                                if len(evg) >= 2 and f4 == 0:
                                    w2.append(evg[-2])
                                tts.append(tr(
                                    pdk[grp % 2][:, f4 * 128:(f4 + 1) * 128],
                                    wnatq[q % 2][:, fs * 128:(fs + 1) * 128],
                                    waits=w2))
                            if _OSUB == "tr":
                                ev0 = op("vector", "tensor_copy",
                                         opfm[:, 0:512], pdk[grp % 2][:, 0:512],
                                         waits=[tts[-1]] + ([evg[-1]] if evg else []))
                                evg.append(ev0)
                                last = ev0
                                grp += 1
                                wout_qprev[q % 2] = ev0
                                continue
                            evs4 = []
                            geng = "vector" if grp % 2 == 0 else "scalar"
                            for f4 in range(4):
                                ft = q * 16 + fg * 4 + f4
                                ew = [tts[-1]]
                                if grp <= 1 and f4 == 0 and t_cache_wout[buf] is not None:
                                    ew.append(t_cache_wout[buf])
                                if geng == "vector":
                                    ev = op("vector", "tensor_scalar",
                                            woasm[buf][:, ft * 128:(ft + 1) * 128],
                                            pdk[grp % 2][:, f4 * 128:(f4 + 1) * 128],
                                            nw_col[:, ft:ft + 1], None, ALU.mult,
                                            waits=ew)
                                else:
                                    ev = op("scalar", "activation",
                                            out=woasm[buf][:, ft * 128:(ft + 1) * 128],
                                            in_=pdk[grp % 2][:, f4 * 128:(f4 + 1) * 128],
                                            func=AF.Identity,
                                            scale=nw_col[:, ft:ft + 1],
                                            waits=ew)
                                evs4.append(ev)
                            last = evs4[-1]
                            if geng == "vector":
                                lastv = last
                            else:
                                lasts = last
                            evg.append(last)
                            grp += 1
                        wout_qprev[q % 2] = evg[-1]
                    if _OSUB not in ("dma", "tr", "ev"):
                        if lastv is not None and lasts is not None:
                            last = op("vector", "engine_nop",
                                      waits=[lastv, lasts])
                        t_cache_wout[buf] = dma("sync", "cwout%d" % buf,
                                                wout_cache[dmt], woasm[buf],
                                                waits=[last])
                    return last
                else:
                    w = list(extra) + ([wprev] if wprev else [])
                    ta = dma("sync", "wo%d" % buf, woasm[buf][:, 0:4096],
                             wout_cache[dmt][:, 0:4096], waits=w)
                    tb = dma("sync", "wo%d" % (buf + 2),
                             woasm[buf][:, 4096:8192],
                             wout_cache[dmt][:, 4096:8192], waits=w)
                    return [ta, tb]

            t_wo_used = [None, None]
            t_opfm_free = [None, None]
            wout_qprev = [None, None]
            t_pt_free = [t_inv, None]
            t_lnm_done = [[] for _ in range(4)]
            t_last_opmm = None
            n_dmt = min(NK, _ODM)
            pend_o = {}
            for it_o in range(n_dmt + 1):
              if it_o < n_dmt:
                dmt = it_o
                buf = dmt % 2
                extra = []
                if dmt <= 1:
                    extra += t_zwaits
                    if pass1:
                        extra.append(t_zmm_last)
                        extra += [t for t in t_cache_win if t is not None]
                wk = acquire_wout(dmt, buf, t_wo_used[buf], extra=extra)
                wkl = wk if isinstance(wk, list) else [wk]
                if _OSUB == "acq":
                    t_wo_used[buf] = wkl[-1]
                    t_lnm_done = None
                    continue
                mm_w = wkl + [t_gate_last, t_pm[buf]]
                t_mmo = None
                for f in range(NFD):
                    t_mmo = mm(pm[buf][:, 0:TT],
                               woasm[buf][:, f * 128:(f + 1) * 128], r0j[:, f, :],
                               waits=mm_w if f == 0 else (), inc=(f == NFD - 1),
                               start=(f == 0), stop=(f == NFD - 1))
                t_wo_used[buf] = t_mmo
                t_last_opmm = t_mmo
                if _OSUB == "mm":
                    t_pm[buf] = op("vector", "tensor_copy",
                                   opfm[:, 0:TT], pm[buf][:, 0:TT],
                                   waits=[t_mmo] + ([t_pm[buf]] if t_pm[buf] else []))
                    continue
                w_ec = [t_mmo]
                if t_opfm_free[dmt % 2] is not None:
                    w_ec.append(t_opfm_free[dmt % 2])
                t_ec = op("vector", "tensor_copy", opfm2[dmt % 2][:, 0:TT],
                          pm[buf][:, 0:TT], waits=w_ec)
                t_pm[buf] = t_ec
                pend_o[dmt] = t_ec
              if it_o >= 1 and _OSUB not in ("acq", "mm"):
                dmt = it_o - 1
                t_ec = pend_o.pop(dmt)
                ptx = dmt % 2
                trs = None
                for ts in range(4):
                    w_tr = []
                    if ts == 0:
                        w_tr = [t_ec]
                        if t_pt_free[ptx] is not None:
                            w_tr.append(t_pt_free[ptx])
                    trs = tr(ptb[ptx][:, ts * 128:(ts + 1) * 128],
                             opfm2[dmt % 2][:, ts * 128:(ts + 1) * 128],
                             waits=w_tr)
                evl = None
                beng = "vector" if dmt % 2 == 0 else "scalar"
                for ts in range(4):
                    dst = lnm[ts][:, dmt * 128:(dmt + 1) * 128]
                    w_e = [trs, t_inv]
                    if dmt == 0 and ts == 0:
                        w_e.append(t_zmm_last)
                    if beng == "vector":
                        evl = op("vector", "tensor_scalar", dst,
                                 ptb[ptx][:, ts * 128:(ts + 1) * 128],
                                 invc[:, ts:ts + 1], None, ALU.mult, waits=w_e)
                    else:
                        evl = op("scalar", "activation", out=dst,
                                 in_=ptb[ptx][:, ts * 128:(ts + 1) * 128],
                                 func=AF.Identity, scale=invc[:, ts:ts + 1],
                                 waits=w_e)
                    t_lnm_done[ts].append(evl)
                t_pt_free[ptx] = evl
                t_opfm_free[dmt % 2] = evl
            if _PH == "o" and last_tile:
                if _OSUB == "acq":
                    wl = [t_wo_used[0] or t_inv, t_wo_used[1] or t_inv]
                elif _OSUB == "mm":
                    wl = [t_pm[0], t_pm[1]]
                else:
                    wl = [t_lnm_done[3][-1], t_lnm_done[0][-1]]
                t_out_last = dma("scalar", "ob", out_h[0:128, :], obuf,
                                 waits=wl)
                break

            t_x1 = []
            t_prev_r = None
            for ts in range(4):
                w = list(t_lnm_done[ts])
                if t_prev_obuf is not None:
                    w.append(t_prev_obuf)
                if t_prev_r is not None:
                    w.append(t_prev_r)
                if ts == 0:
                    w += [t_mp[0], t_mp[1], t_gate_last]
                tn = dma("scalar", "xr", xres,
                         hs[c0 + ts * 128: c0 + (ts + 1) * 128, :], waits=w)
                t_r = op("vector", "scalar_tensor_tensor", lnm[ts], xres, 2.0,
                         lnm[ts], ALU.mult, ALU.add,
                         waits=[tn] + list(t_lnm_done[ts]))
                t_st1 = op("scalar", "activation", out=ln2b, in_=lnm[ts],
                           func=AF.Identity, accum_out=smallf[:, 0:1], waits=[t_r])
                t_st2 = op("scalar", "activation", out=ln2b, in_=lnm[ts],
                           func=AF.Square, accum_out=smallf[:, 1:2], waits=[t_st1])
                t_mu = op("vector", "tensor_scalar_mul", smallf[:, 2:3],
                          smallf[:, 0:1], 1.0 / DM, waits=[t_st2])
                t_m2 = op("vector", "tensor_scalar_mul", smallf[:, 3:4],
                          smallf[:, 1:2], 1.0 / DM, waits=[t_mu])
                t_nm = op("vector", "tensor_scalar_mul", smallf[:, 4:5],
                          smallf[:, 2:3], -1.0, waits=[t_m2])
                t_va = op("vector", "scalar_tensor_tensor", smallf[:, 5:6],
                          smallf[:, 2:3], smallf[:, 4:5], smallf[:, 3:4],
                          ALU.mult, ALU.add, waits=[t_nm])
                t_l1 = op("scalar", "activation", out=smallf[:, 6:7],
                          in_=smallf[:, 5:6], func=AF.Ln, bias=EPS, waits=[t_va])
                t_iv = op("scalar", "activation", out=smallf[:, 6:7],
                          in_=smallf[:, 6:7], func=AF.Exp, scale=-0.5, waits=[t_l1])
                t_n2 = op("vector", "tensor_tensor", smallf[:, 7:8],
                          smallf[:, 2:3], smallf[:, 6:7], ALU.mult, waits=[t_iv])
                t_n3 = op("vector", "tensor_scalar_mul", smallf[:, 7:8],
                          smallf[:, 7:8], -1.0, waits=[t_n2])
                t_x1t = op("vector", "tensor_scalar", lnm[ts], lnm[ts],
                           smallf[:, 6:7], smallf[:, 7:8], ALU.mult, ALU.add,
                           waits=[t_n3])
                t_x1.append(t_x1t)
                t_prev_r = t_r
            t_x1fm_ev = []
            for ts in range(4):
                for kg in range(8):
                    tts = []
                    for k4 in range(4):
                        k = kg * 4 + k4
                        w2 = []
                        if k4 == 0:
                            w2 = [t_x1[ts]]
                            if ts == 0 and kg == 0:
                                w2 += [t_last_opmm, t_pt_free[0], t_pt_free[1]]
                                if pass1:
                                    w2 += [t for t in t_cache_wout if t is not None]
                            if len(t_x1fm_ev) >= 2:
                                w2.append(t_x1fm_ev[-2])
                        tts.append(tr(ptb[kg % 2][:, k4 * 128:(k4 + 1) * 128],
                                      lnm[ts][:, k * 128:(k + 1) * 128], waits=w2))
                    t_x1fm_ev.append(op(
                        "vector", "tensor_copy",
                        x1fmk[:, kg * 4:(kg + 1) * 4, ts * 128:(ts + 1) * 128],
                        ptb[kg % 2].rearrange("p (k c) -> p k c", c=128)[:, 0:4, :],
                        waits=tts))
            t_x1fm = t_x1fm_ev[-1]

            t_fl = dma("sync", "ffn", acat, ffn_d[:, 0:2048],
                       waits=[t_last_opmm, t_zmm_last])
            t_fl = dma("sync", "ffn", bcat, ffn_d[0:64, 2048:6144], waits=[t_fl])
            t_fl = dma("sync", "ffn", wrt, ffn_d[:, 6144:6272], waits=[t_fl])
            t_rmm = None
            for k in range(NK):
                t_rmm = mm(pm[0][0:4, 0:TT], wrt[:, k * 4:(k + 1) * 4],
                           x1fmk[:, k, :],
                           waits=[t_x1fm, t_pm[0], t_fl] if k == 0 else (),
                           inc=(k == NK - 1), start=(k == 0), stop=(k == NK - 1))
            t_rev = op("vector", "tensor_copy", rraw, pm[0][0:4, 0:TT],
                       waits=[t_rmm])
            t_pm[0] = t_rev
            t_lt = None
            for ts in range(4):
                t_lt = tr(pt[0][:, 320 + ts * 4:320 + ts * 4 + 4],
                          rraw[0:4, ts * 128:(ts + 1) * 128],
                          waits=[t_rev, t_x1fm_ev[-2]] if ts == 0 else ())
            t_log = op("vector", "tensor_copy", logits, pt[0][:, 320:336],
                       waits=[t_lt])
            t_pt0_free = t_log
            t_gts = None
            for ts in range(4):
                lg = logits[:, ts * 4:(ts + 1) * 4]
                t_m1 = op("vector", "reduce_max", smallf[:, 16:17], lg, AX.X,
                          waits=[t_log] if ts == 0 else [t_gts])
                t_mn = op("vector", "tensor_scalar_mul", smallf[:, 17:18],
                          smallf[:, 16:17], -1.0, waits=[t_m1])
                t_e = op("scalar", "activation", out=gatesb[:, ts * 4:(ts + 1) * 4],
                         in_=lg, func=AF.Exp, bias=smallf[:, 17:18], waits=[t_mn])
                t_cn = None
                for e in range(4):
                    bce = lg[:, e:e + 1].broadcast_to([128, 4])
                    t_gt = op("vector", "tensor_tensor", cmp4[:, 0:4], lg, bce,
                              ALU.is_gt, waits=[t_e] if e == 0 else [t_cn])
                    t_eq = op("vector", "tensor_tensor", cmp4[:, 4:8], lg, bce,
                              ALU.is_equal, waits=[t_gt])
                    t_ej = op("vector", "tensor_tensor", cmp4[:, 4:8], cmp4[:, 4:8],
                              jlt[:, e * 4:(e + 1) * 4], ALU.mult, waits=[t_eq])
                    t_ad = op("vector", "tensor_tensor", cmp4[:, 0:4], cmp4[:, 0:4],
                              cmp4[:, 4:8], ALU.add, waits=[t_ej])
                    t_cn = op("vector", "reduce_sum", cnt4[:, e:e + 1],
                              cmp4[:, 0:4], AX.X, waits=[t_ad])
                t_mk = op("vector", "tensor_single_scalar", cnt4, cnt4, 2.0,
                          ALU.is_lt, waits=[t_cn])
                t_em = op("vector", "tensor_tensor", gatesb[:, ts * 4:(ts + 1) * 4],
                          gatesb[:, ts * 4:(ts + 1) * 4], cnt4, ALU.mult,
                          waits=[t_mk])
                t_sm = op("vector", "reduce_sum", smallf[:, 18:19],
                          gatesb[:, ts * 4:(ts + 1) * 4], AX.X, waits=[t_em])
                t_rc = op("vector", "reciprocal", smallf[:, 19:20],
                          smallf[:, 18:19], waits=[t_sm])
                t_gts = op("vector", "tensor_scalar_mul",
                           gatesb[:, ts * 4:(ts + 1) * 4],
                           gatesb[:, ts * 4:(ts + 1) * 4], smallf[:, 19:20],
                           waits=[t_rc])
            t_gT = None
            for ts in range(4):
                t_gT = tr(pt[0][0:4, ts * 128:(ts + 1) * 128],
                          gatesb[:, ts * 4:(ts + 1) * 4],
                          waits=[t_gts, t_pt0_free] if ts == 0 else ())
            t_gTe = op("vector", "tensor_copy", gatesT, pt[0][0:4, 0:512],
                       waits=[t_gT])
            t_gTb = op("vector", "tensor_copy", gatesTb, gatesT, waits=[t_gTe])
            t_grep = mm(py[0][0:64, 0:TT], e4, gatesTb,
                        waits=[t_gTb, t_sup[0], t_yev[0]])
            t_mmm = None
            for k in range(NK):
                t_mmm = mm(pm[1][0:64, 0:TT], acat[:, k * 64:(k + 1) * 64],
                           x1fmk[:, k, :],
                           waits=[t_x1fm, t_pm[1], t_fl] if k == 0 else (),
                           inc=(k == NK - 1), start=(k == 0), stop=(k == NK - 1))
            t_me = op("vector", "tensor_copy", midb, pm[1][0:64, 0:TT],
                      waits=[t_mmm])
            t_pm[1] = t_me
            t_gm = op("vector", "tensor_tensor", gmidb, midb, py[0][0:64, 0:TT],
                      ALU.mult, waits=[t_me, t_grep])
            t_bankfree = [t_sup[1], None]
            t_ln2_add = None
            for ts in range(4):
                t_adds = []
                for obk in range(8):
                    bank = py[1] if obk % 2 == 0 else pdk[1]
                    w = [t_gm]
                    if t_bankfree[obk % 2] is not None:
                        w.append(t_bankfree[obk % 2])
                    tmm2 = mm(bank[:, 0:TT], gmidb[:, ts * 128:(ts + 1) * 128],
                              bcat[:, obk * 512:(obk + 1) * 512], waits=w)
                    t_last_deltamm = tmm2
                    w_ad = [tmm2]
                    if ts == 0 and obk == 0:
                        w_ad.append(t_st2 if False else t_x1[3])
                    tadd = op("vector", "tensor_tensor",
                              ln2b[:, obk * 512:(obk + 1) * 512], bank[:, 0:TT],
                              lnm[ts][:, obk * 512:(obk + 1) * 512], ALU.add,
                              waits=w_ad)
                    t_bankfree[obk % 2] = tadd
                    t_adds.append(tadd)
                w_s = t_adds
                if t_prev_obuf is not None:
                    w_s = t_adds + [t_prev_obuf]
                t_u1 = op("scalar", "activation", out=r0[:, 0:4096], in_=ln2b,
                          func=AF.Identity, accum_out=smallf[:, 20:21], waits=w_s)
                t_u2 = op("scalar", "activation", out=r0[:, 0:4096], in_=ln2b,
                          func=AF.Square, accum_out=smallf[:, 21:22], waits=[t_u1])
                t_mu = op("vector", "tensor_scalar_mul", smallf[:, 22:23],
                          smallf[:, 20:21], 1.0 / DM, waits=[t_u2])
                t_m2 = op("vector", "tensor_scalar_mul", smallf[:, 23:24],
                          smallf[:, 21:22], 1.0 / DM, waits=[t_mu])
                t_nm = op("vector", "tensor_scalar_mul", smallf[:, 24:25],
                          smallf[:, 22:23], -1.0, waits=[t_m2])
                t_va = op("vector", "scalar_tensor_tensor", smallf[:, 25:26],
                          smallf[:, 22:23], smallf[:, 24:25], smallf[:, 23:24],
                          ALU.mult, ALU.add, waits=[t_nm])
                t_l2 = op("scalar", "activation", out=smallf[:, 26:27],
                          in_=smallf[:, 25:26], func=AF.Ln, bias=EPS, waits=[t_va])
                t_i2 = op("scalar", "activation", out=smallf[:, 26:27],
                          in_=smallf[:, 26:27], func=AF.Exp, scale=-0.5,
                          waits=[t_l2])
                t_p2 = op("vector", "tensor_tensor", smallf[:, 27:28],
                          smallf[:, 22:23], smallf[:, 26:27], ALU.mult,
                          waits=[t_i2])
                t_p3 = op("vector", "tensor_scalar_mul", smallf[:, 27:28],
                          smallf[:, 27:28], -1.0, waits=[t_p2])
                t_x2 = op("vector", "tensor_scalar", obuf, ln2b,
                          smallf[:, 26:27], smallf[:, 27:28], ALU.mult, ALU.add,
                          waits=[t_p3])
                t_od = dma("scalar", "ob",
                           out_h[c0 + ts * 128: c0 + (ts + 1) * 128, :],
                           obuf, waits=[t_x2])
                t_prev_obuf = t_od
                t_out_last = t_od

            tok_r0_free = t_u2
            tok_r1_free = t_x1fm
            tok_r5_free = t_x2
            tok_r2_free = t_mmm
            tok_r3_free = t_last_deltamm
            tok_r6_free = t_out_last
            tok_pm = [t_pm[0], t_pm[1]]

        def f_fin(eng):
            return eng.nop()
        P.add("scalar", f_fin, waits=[t_out_last])

        for s, v in P.cnt.items():
            assert v < 60000, f"semaphore {s} overflows: {v}"

        def replay(name):
            def body(eng):
                waited = {}
                for (fn, waits, incinfo) in P.steps[name]:
                    for (s, v) in waits:
                        if waited.get(s, 0) < v:
                            eng.wait_ge(sems[s], v)
                            waited[s] = v
                    ins = fn(eng)
                    if incinfo is not None:
                        s, n = incinfo
                        ins.then_inc(sems[s], n)
            return body

        blk.sync(replay("sync"))
        blk.gpsimd(replay("gpsimd"))
        blk.scalar(replay("scalar"))
        blk.vector(replay("vector"))
        blk.tensor(replay("tensor"))

    return nc


def kernel(**inputs):
    hs_full = np.ascontiguousarray(inputs["L0_attn_hidden_states"], dtype=np.float32)
    nc = build_program()
    wkeys = ["L0_attn_in_proj_w", "L0_attn_conv_w", "L0_attn_conv_b",
             "L0_attn_dt_bias", "L0_attn_A_log", "L0_attn_D", "L0_attn_norm_w",
             "L0_attn_out_proj_w", "L0_ffn_router_w", "L0_ffn_A", "L0_ffn_B"]
    in_maps = []
    for c in range(8):
        m = {"hs": np.ascontiguousarray(hs_full[c])}
        for k in wkeys:
            m[k] = np.ascontiguousarray(inputs[k], dtype=np.float32)
        in_maps.append(m)
    res = run_bass_kernel_spmd(nc, in_maps, core_ids=list(range(8)))
    return np.stack([res.results[i]["out"] for i in range(8)], axis=0)
